# version 06
# baseline (speedup 1.0000x reference)
import numpy as np
from contextlib import ExitStack

import concourse.bass as bass
import concourse.mybir as mybir

F32 = mybir.dt.float32
BF16 = mybir.dt.bfloat16
I32 = mybir.dt.int32
AF = mybir.ActivationFunctionType
ALU = mybir.AluOpType
AX = mybir.AxisListType

ENGINES = ("sync", "scalar", "vector", "tensor", "gpsimd")


def vp(pairs):
    return mybir.VecI64Pair([list(x) for x in pairs])


def apv(tensor, offset, pairs):
    ap = tensor.ap().copy()
    ap.offset = offset
    ap.ap = vp(pairs)
    return ap


def sbv(tensor, free_size, nparts, offset, free_pairs):
    ap = tensor.ap().copy()
    ap.offset = offset
    ap.ap = vp([[free_size, nparts]] + list(free_pairs))
    return ap


def ceil_div(a, b):
    return (a + b - 1) // b


def ntiles(T, n=512):
    out = []
    s = 0
    while s < T:
        out.append((s, min(n, T - s)))
        s += n
    if len(out) >= 2 and out[-1][1] < 128:
        s0, n0 = out[-2]
        s1, n1 = out[-1]
        tot = n0 + n1
        a = (tot + 1) // 2
        a += a % 2
        out[-2] = (s0, a)
        out[-1] = (s0 + a, tot - a)
    return out


class Prog:
    def __init__(self, name="prog"):
        self.nc = bass.Bass("TRN2", target_bir_lowering=False)
        self.ctx = ExitStack()
        self.q = {e: [] for e in ENGINES}
        self.sems = {}
        self.plan = {}
        self.waited = {e: {} for e in ENGINES}
        self._uid = 0
        self.store_toks = []
        self._built = False
        self._consts = {}

    def uid(self):
        self._uid += 1
        return self._uid

    def sem(self, name):
        if name not in self.sems:
            self.sems[name] = self.ctx.enter_context(self.nc.semaphore(name))
            self.plan[name] = 0
        return self.sems[name]

    def sbuf(self, name, shape, dtype):
        return self.ctx.enter_context(self.nc.sbuf_tensor(name, shape, dtype))

    def psum(self, name, shape, dtype=F32):
        return self.ctx.enter_context(self.nc.psum_tensor(name, shape, dtype))

    def din(self, name, shape, dtype=F32):
        return self.nc.dram_tensor(name, list(shape), dtype, kind="ExternalInput")

    def dout(self, name, shape, dtype=F32):
        return self.nc.dram_tensor(name, list(shape), dtype, kind="ExternalOutput")

    def dint(self, name, shape, dtype=F32):
        return self.nc.dram_tensor(name, list(shape), dtype, kind="Internal")

    def _waits(self, eng, wait):
        todo = []
        for t in wait:
            if t is None:
                continue
            s, v, src = t
            if self.waited[eng].get(s, 0) >= v:
                continue
            self.waited[eng][s] = v
            todo.append((s, v))
        return todo

    def op(self, eng, fn, wait=(), sig=True):
        return self.ops(eng, [fn], wait=wait, sig=sig)

    def ops(self, eng, fns, wait=(), sig=True, chain=False):
        sname = f"c_{eng}"
        self.sem(sname)
        todo = self._waits(eng, wait)
        tok = None
        if chain and len(fns) > 1:
            n_int = len(fns) - 1
            base = self.plan[sname]
            self.plan[sname] += n_int
            if sig:
                self.plan[sname] += 1
                tok = (sname, self.plan[sname], eng)

            def runc(h, todo=todo, fns=tuple(fns), sig=sig, sname=sname, base=base):
                for (s, v) in todo:
                    h.wait_ge(self.sems[s], v)
                for i, fn in enumerate(fns):
                    if i > 0:
                        h.wait_ge(self.sems[sname], base + i)
                    ins = fn(h)
                    if i < len(fns) - 1 or sig:
                        ins.then_inc(self.sems[sname], 1)
            self.q[eng].append(runc)
            self.waited[eng][sname] = max(self.waited[eng].get(sname, 0), base + n_int)
            return tok
        if sig:
            self.plan[sname] += 1
            tok = (sname, self.plan[sname], eng)

        def run(h, todo=todo, fns=tuple(fns), sig=sig, sname=sname):
            for (s, v) in todo:
                h.wait_ge(self.sems[s], v)
            ins = None
            for fn in fns:
                ins = fn(h)
            if sig:
                ins.then_inc(self.sems[sname], 1)
        self.q[eng].append(run)
        return tok

    DMA_SEM_ROT = 6

    def dma(self, eng, out, in_, wait=(), store=False):
        rot = getattr(self, "_dma_rot", None)
        if rot is None:
            rot = self._dma_rot = {}
        k = rot.get(eng, 0)
        rot[eng] = (k + 1) % self.DMA_SEM_ROT
        sname = f"d_{eng}{k}"
        self.sem(sname)
        prev = self.plan[sname] if sname in self.plan else 0
        todo = self._waits(eng, list(wait) + ([(sname, prev, None)] if prev else []))
        self.plan[sname] += 16
        tok = (sname, self.plan[sname], None)

        def run(h, todo=todo, out=out, in_=in_, sname=sname):
            for (s, v) in todo:
                h.wait_ge(self.sems[s], v)
            h.dma_start(out=out, in_=in_).then_inc(self.sems[sname], 16)
        self.q[eng].append(run)
        if store:
            self.store_toks.append(tok)
        return tok

    def wait(self, eng, toks):
        todo = self._waits(eng, toks)
        if todo:
            def run(h, todo=todo):
                for (s, v) in todo:
                    h.wait_ge(self.sems[s], v)
            self.q[eng].append(run)

    def collective(self, kind, op, groups, ins, outs, wait=()):
        from concourse.aot_env import allow_ici_all_to_all
        sname = "d_cc"
        self.sem(sname)
        todo = self._waits("gpsimd", wait)
        self.plan[sname] += 16
        tok = (sname, self.plan[sname], None)

        def run(h, todo=todo, kind=kind, op=op, groups=groups, ins=ins, outs=outs,
                sname=sname):
            for (s, v) in todo:
                h.wait_ge(self.sems[s], v)
            with allow_ici_all_to_all():
                h.collective_compute(kind, op, replica_groups=groups, ins=ins,
                                     outs=outs).then_inc(self.sems[sname], 16)
        self.q["gpsimd"].append(run)
        return tok

    def consts(self):
        if "ones" in self._consts:
            return self._consts
        c = {}
        c["ones_col_bf"] = self.sbuf("ones_col_bf", [128, 1], BF16)
        c["ones_col_f32"] = self.sbuf("ones_col_f32", [128, 1], F32)
        c["ones_row_bf"] = self.sbuf("ones_row_bf", [1, 128], BF16)
        c["ones_row_f32"] = self.sbuf("ones_row_f32", [1, 128], F32)
        t1 = self.op("vector", lambda v: v.memset(c["ones_col_bf"][:], 1.0))
        t2 = self.op("vector", lambda v: v.memset(c["ones_col_f32"][:], 1.0))
        t3 = self.op("vector", lambda v: v.memset(c["ones_row_bf"][:], 1.0))
        t4 = self.op("vector", lambda v: v.memset(c["ones_row_f32"][:], 1.0))
        c["tok"] = [t1, t2, t3, t4]
        c["ones"] = True
        self._consts = c
        return c

    def finish(self):
        self.wait("sync", self.store_toks)

    def build(self):
        assert not self._built
        self._built = True
        self.ctx.enter_context(self.nc.allow_low_precision(
            reason="bf16 scratch for softmax bounds; real accumulations are fp32 PSUM"))
        block = self.ctx.enter_context(self.nc.Block())
        for e in ENGINES:
            if not self.q[e]:
                continue

            def make(e):
                def body(h):
                    for fn in self.q[e]:
                        fn(h)
                return body
            getattr(block, e)(make(e))
        self.ctx.close()
        return self.nc


class PsumPool:
    def __init__(self, p, n=6, name="pp", width=512):
        self.p = p
        self.t = [p.psum(f"{name}{i}", [128, width], F32) for i in range(n)]
        self.last_reader = [None] * n
        self.i = 0
        self.n = n

    def acquire(self):
        i = self.i
        self.i = (self.i + 1) % self.n
        tok = self.last_reader[i]
        self.last_reader[i] = None
        if tok is None:
            return i, self.t[i], []
        return i, self.t[i], (list(tok) if isinstance(tok, list) else [tok])

    def release(self, i, reader_tok):
        self.last_reader[i] = reader_tok



def load_weight(p, name, w_dram, K, M, wait=(), eng="gpsimd"):
    Kc = ceil_div(K, 128)
    t = p.sbuf(name, [128, Kc * M], BF16)
    toks = []
    for kc in range(Kc):
        rows = min(128, K - kc * 128)
        src = apv(w_dram, kc * 128 * M, [[M, rows], [1, M]])
        toks.append(p.dma(eng, t[0:rows, kc * M:(kc + 1) * M], src, wait=wait))
    return t, toks



def emit_ln(p, pool, x_get, xn_sb, T, D=1024, tag="ln",
            g_col=None, b_col=None, aff_toks=(), eps=1e-5, out_T=None):
    C = ceil_div(D, 128)
    out_T = out_T or T
    c = p.consts()
    u = p.uid()
    sqa = p.sbuf(f"{tag}{u}_sqa", [128, 512], BF16)
    sqb = p.sbuf(f"{tag}{u}_sqb", [128, 512], BF16)
    rows = p.sbuf(f"{tag}{u}_rows", [1, 1536], F32)
    rows_bf = p.sbuf(f"{tag}{u}_rbf", [1, 1024], BF16)
    bc = p.sbuf(f"{tag}{u}_bc", [128, 1024], BF16)
    t1 = p.sbuf(f"{tag}{u}_t1", [128, 512], BF16)
    out_toks = []
    sq_tok = {0: None, 1: None}
    last = None
    for (s0, n) in ntiles(T, 512):
        bi, ps, w8 = pool.acquire()
        bi2, ps2, w82 = pool.acquire()
        xaps = []
        for ci in range(C):
            xaps.append(x_get(ci, s0, n))
        fns = []
        for ci in range(C):
            r = min(128, D - ci * 128)
            xap, xw = xaps[ci]
            fns.append(lambda t, a=ps[0:1, 0:n], l=c["ones_col_f32"][0:r, 0:1], rr=xap,
                       st=(ci == 0), sp=(ci == C - 1): t.matmul(a, l, rr, start=st, stop=sp))
        allxw = [t for (_, ws) in xaps for t in ws]
        tok_sum = p.ops("tensor", fns, wait=allxw + c["tok"] + w8)
        tok_mm = None
        for ci in range(C):
            r = min(128, D - ci * 128)
            sq = sqa if ci % 2 == 0 else sqb
            xap, xw = xaps[ci]
            wsq = [sq_tok[ci % 2]] if sq_tok[ci % 2] else []
            tsq = p.op("scalar",
                       lambda s, o=sq[0:r, 0:n], i=xap: s.activation(out=o, in_=i, func=AF.Square),
                       wait=list(xw) + wsq + (w82 if ci < 2 else []))
            tok_mm = p.op("tensor",
                          lambda t, a=ps2[0:1, 0:n], l=c["ones_col_bf"][0:r, 0:1],
                          rr=sq[0:r, 0:n], st=(ci == 0), sp=(ci == C - 1):
                          t.matmul(a, l, rr, start=st, stop=sp),
                          wait=[tsq] + c["tok"])
            sq_tok[ci % 2] = tok_mm
        nm = rows[0:1, 0:n]
        ms = rows[0:1, 512:512 + n]
        sc = rows[0:1, 1024:1024 + n]
        tA = p.op("vector", lambda v, o=nm, i=ps[0:1, 0:n]: v.tensor_scalar_mul(o, i, -1.0 / D),
                  wait=[tok_sum])
        tB = p.ops("vector", [
            lambda v, o=ms, i=ps2[0:1, 0:n]: v.tensor_scalar_mul(o, i, 1.0 / D),
            lambda v, o=sc, i=nm: v.tensor_mul(o, i, i),
            lambda v, o=ms, i=ms, j=sc: v.tensor_sub(o, i, j),
            lambda v, o=ms, i=ms: v.tensor_scalar_add(o, i, eps),
        ], wait=[tok_mm, tA], chain=True)
        pool.release(bi, tB)
        pool.release(bi2, tB)
        tC = p.op("scalar", lambda s, o=sc, i=ms: s.activation(out=o, in_=i, func=AF.Sqrt),
                  wait=[tB])
        tD = p.op("vector", lambda v, o=ms, i=sc: v.reciprocal(o, i), wait=[tC])
        tE = p.ops("vector", [
            lambda v, o=rows_bf[0:1, 0:n], i=nm: v.tensor_copy(o, i),
            lambda v, o=rows_bf[0:1, 512:512 + n], i=ms: v.tensor_copy(o, i),
        ], wait=[tD])
        bi3, ps3, w83 = pool.acquire()
        tF = p.op("tensor", lambda t, a=ps3[0:128, 0:n], l=c["ones_row_bf"][0:1, 0:128],
                  rr=rows_bf[0:1, 0:n]: t.matmul(a, l, rr, start=True, stop=True),
                  wait=[tE] + w83)
        bi4, ps4, w84 = pool.acquire()
        tG = p.op("tensor", lambda t, a=ps4[0:128, 0:n], l=c["ones_row_bf"][0:1, 0:128],
                  rr=rows_bf[0:1, 512:512 + n]: t.matmul(a, l, rr, start=True, stop=True),
                  wait=[tE] + w84)
        tH = p.op("vector", lambda v, o=bc[0:128, 0:n], i=ps3[0:128, 0:n]: v.tensor_copy(o, i),
                  wait=[tF])
        tI = p.op("vector", lambda v, o=bc[0:128, 512:512 + n], i=ps4[0:128, 0:n]: v.tensor_copy(o, i),
                  wait=[tG])
        pool.release(bi3, tH)
        pool.release(bi4, tI)
        for ci in range(C):
            r = min(128, D - ci * 128)
            xap, xw = xaps[ci]
            xout = xn_sb[0:r, ci * out_T + s0: ci * out_T + s0 + n]
            fns = [lambda v, o=t1[0:r, 0:n], i=xap, j=bc[0:r, 0:n]: v.tensor_add(o, i, j)]
            if g_col is not None:
                fns.append(lambda v, o=t1[0:r, 0:n], i=t1[0:r, 0:n], j=bc[0:r, 512:512 + n]:
                           v.tensor_mul(o, i, j))
                fns.append(lambda v, o=xout, i=t1[0:r, 0:n], s1=g_col[0:r, ci:ci + 1],
                           s2=b_col[0:r, ci:ci + 1]:
                           v.tensor_scalar(o, i, s1, s2, op0=ALU.mult, op1=ALU.add))
            else:
                fns.append(lambda v, o=xout, i=t1[0:r, 0:n], j=bc[0:r, 512:512 + n]:
                           v.tensor_mul(o, i, j))
            last = p.ops("vector", fns, wait=[tH, tI, last] + list(aff_toks) + list(xw),
                         chain=True)
        out_toks.append(last)
        if hasattr(x_get, "note_reader"):
            for ci in range(C):
                x_get.note_reader(ci, s0, n, tok_sum)
                x_get.note_reader(ci, s0, n, sq_tok[1] or sq_tok[0])
                x_get.note_reader(ci, s0, n, last)
    return out_toks


def make_x_loader(p, x_dram, Tall, D=1024, nbuf=3, tag="xload", eng="sync",
                  dtype=None):
    C = ceil_div(D, 128)
    dtype = dtype or F32
    bufs = [p.sbuf(f"{tag}{p.uid()}_{i}", [128, 512], dtype) for i in range(nbuf)]
    state = {"i": 0, "cache": {}, "readers": {}}

    def x_get(ci, s0, n, record_reader=None):
        key = (ci, s0, n)
        if key in state["cache"]:
            b_i, tok = state["cache"][key]
            return bufs[b_i][0:min(128, D - ci * 128), 0:n], [tok]
        b_i = state["i"]
        state["i"] = (state["i"] + 1) % nbuf
        r = min(128, D - ci * 128)
        src = apv(x_dram, ci * 128 * Tall + s0, [[Tall, r], [1, n]])
        war = state["readers"].pop(b_i, [])
        tok = p.dma(eng, bufs[b_i][0:r, 0:n], src, wait=war)
        state["cache"] = {k: v for k, v in state["cache"].items() if v[0] != b_i}
        state["cache"][key] = (b_i, tok)
        return bufs[b_i][0:r, 0:n], [tok]

    def note_reader(ci, s0, n, tok):
        if tok is None:
            return
        key = (ci, s0, n)
        if key in state["cache"]:
            b_i, _ = state["cache"][key]
            state["readers"].setdefault(b_i, []).append(tok)
    x_get.note_reader = note_reader
    return x_get



def emit_proj(p, pool, wsb, w_toks, x_sb, x_off, x_T, x_toks, K, M, T,
              out_cb, n_tile=512, kc_list=None, x_rows=None):
    Kc = ceil_div(K, 128)
    Mc = ceil_div(M, 128)
    kcs = kc_list if kc_list is not None else list(range(Kc))
    toks = []
    for mc in range(Mc):
        mrows = min(128, M - mc * 128)
        for (s0, n) in ntiles(T, n_tile):
            bi, ps, w8 = pool.acquire()
            fns = []
            for i, kc in enumerate(kcs):
                r = min(128, K - i * 128)
                l = wsb[0:r, kc * M + mc * 128: kc * M + mc * 128 + mrows]
                rr = x_sb[0:r, kc * x_T + x_off + s0: kc * x_T + x_off + s0 + n]
                fns.append(lambda t, a=ps[0:mrows, 0:n], l=l, rr=rr,
                           st=(i == 0), sp=(i == len(kcs) - 1): t.matmul(a, l, rr, start=st, stop=sp))
            tok_mm = p.ops("tensor", fns, wait=list(x_toks) + list(w_toks) + w8)
            rd = out_cb(mc, mrows, s0, n, ps, tok_mm)
            pool.release(bi, rd)
            toks.append(rd)
    return toks



def emit_attn_head(p, pool, *, kaug_h, k_toks, qaug_h, q_toks, Tq, Tkv,
                   wlo, whi, scale, mask_sb, mask_offs, mask_toks,
                   vtm, vt_stride, vt_total, v_off, v_toks,
                   pt_buf, pt_stride, o_cb, scratch_war=(), score_pool=None,
                   kt_skip=(), mask_mode="mul"):
    spool = score_pool or pool
    nkt = Tkv // 128
    pt_toks = [None] * nkt
    qwin = []
    for kt in range(nkt):
        q_lo = max(0, kt * 128 - whi)
        q_hi = min(Tq, kt * 128 + 127 - wlo + 1)
        qwin.append((q_lo, q_hi))
    for kt in range(nkt):
        if kt in kt_skip:
            continue
        q_lo, q_hi = qwin[kt]
        w = q_hi - q_lo
        if w <= 0:
            continue
        assert w <= pt_stride, (kt, w)
        bi, ps, w8 = spool.acquire()
        fns = []
        for (o0, nn) in ntiles(w, 512):
            fns.append(lambda t, a=ps[0:128, o0:o0 + nn],
                       l=kaug_h[0:65, kt * 128:kt * 128 + 128],
                       rr=qaug_h[0:65, q_lo + o0:q_lo + o0 + nn]:
                       t.matmul(a, l, rr, start=True, stop=True))
        tok_mm = p.ops("tensor", fns, wait=list(k_toks) + list(q_toks) + w8)
        pt_ap = pt_buf[0:128, kt * pt_stride: kt * pt_stride + w]
        if mask_mode == "add_pre" and mask_offs[kt] is not None:
            moff, mw = mask_offs[kt]
            assert mw == w, (kt, mw, w)
            tok_mm = p.op("vector", lambda v, o=ps[0:128, 0:w], i=ps[0:128, 0:w],
                          m=mask_sb[0:128, moff:moff + w]: v.tensor_add(o, i, m),
                          wait=[tok_mm] + list(mask_toks))
        tok_e = p.op("scalar", lambda s, o=pt_ap, i=ps[0:128, 0:w], sc=scale:
                     s.activation(out=o, in_=i, func=AF.Exp, scale=sc),
                     wait=[tok_mm] + list(scratch_war))
        spool.release(bi, tok_e)
        if mask_mode == "mul" and mask_offs[kt] is not None:
            moff, mw = mask_offs[kt]
            assert mw == w, (kt, mw, w)
            tok_k = p.op("vector", lambda v, o=pt_ap, i=pt_ap,
                         m=mask_sb[0:128, moff:moff + w]: v.tensor_mul(o, i, m),
                         wait=[tok_e] + list(mask_toks))
            pt_toks[kt] = tok_k
        else:
            pt_toks[kt] = tok_e
    readers = []
    for (c0, n) in ntiles(Tq, 512):
        kt_lo = max(0, (c0 + wlo) // 128)
        kt_hi = min(nkt - 1, (c0 + n - 1 + whi) // 128)
        kts = [kt for kt in range(kt_lo, kt_hi + 1)
               if kt not in kt_skip and qwin[kt][1] > qwin[kt][0]
               and min(qwin[kt][1], c0 + n) > max(qwin[kt][0], c0)]
        bi, psB, w8 = pool.acquire()
        fns = []
        wts = []
        spans = {kt: (max(qwin[kt][0], c0), min(qwin[kt][1], c0 + n)) for kt in kts}
        cuts = sorted({v for kt in kts for v in spans[kt]})
        for a0, a1 in zip(cuts[:-1], cuts[1:]):
            cover = [kt for kt in kts if spans[kt][0] <= a0 and spans[kt][1] >= a1]
            for j, kt in enumerate(cover):
                q_lo = qwin[kt][0]
                lhs = vtm[0:128, kt * vt_stride + v_off: kt * vt_stride + v_off + 128]
                rr = pt_buf[0:128, kt * pt_stride + (a0 - q_lo):
                            kt * pt_stride + (a1 - q_lo)]
                a = psB[0:128, a0 - c0:a1 - c0]
                fns.append(lambda t, a=a, l=lhs, rr=rr, st=(j == 0),
                           sp=(j == len(cover) - 1):
                           t.matmul(a, l, rr, start=st, stop=sp))
                wts.append(pt_toks[kt])
        tok_av = p.ops("tensor", fns, wait=wts + list(v_toks) + w8)
        rd = o_cb(c0, n, psB, tok_av)
        pool.release(bi, rd)
        readers.append(rd)
    return readers


def attn_geometry(Tq, Tkv, wlo, whi, force_lo=0):
    nkt = Tkv // 128
    win = whi - wlo + 128
    offs = [None] * nkt
    interior = []
    rest = []
    for kt in range(nkt):
        q_lo = max(0, kt * 128 - whi)
        q_hi = min(Tq, kt * 128 + 127 - wlo + 1)
        w = q_hi - q_lo
        if w <= 0:
            continue
        ki = np.arange(128)[:, None] + kt * 128
        qi = np.arange(q_lo, q_hi)[None, :]
        m = (ki >= qi + wlo) & (ki <= qi + whi)
        if m.all() and kt * 128 >= force_lo:
            continue
        if (q_lo == kt * 128 - whi and w == win and kt * 128 >= force_lo):
            interior.append(kt)
        else:
            rest.append((kt, w))
    total = 0
    if interior:
        for kt in interior:
            offs[kt] = (0, win)
        total = win
    for (kt, w) in rest:
        offs[kt] = (total, w)
        total += w
    return offs, max(total, 1)


def attn_mask_host(Tq, Tkv, wlo, whi, valid_kv_lo, force_lo=0, additive=False):
    offs, total = attn_geometry(Tq, Tkv, wlo, whi, force_lo)
    mask = np.zeros((128, total), np.float32)
    for kt, o in enumerate(offs):
        if o is None:
            continue
        off, w = o
        q_lo = max(0, kt * 128 - whi)
        ki = np.arange(128)[:, None] + kt * 128
        qi = np.arange(q_lo, q_lo + w)[None, :]
        m = (ki >= qi + wlo) & (ki <= qi + whi) & (ki >= valid_kv_lo)
        if additive:
            mask[:, off:off + w] = np.where(m, 0.0, -1e30).astype(np.float32)
        else:
            mask[:, off:off + w] = m.astype(np.float32)
    return mask


def emit_proj_stream(p, pool, w_dram, Wcols, x_sb, x_off, x_T, x_toks, K, M, T,
                     out_cb, n_tile=512, tag="wps"):
    Kc = ceil_div(K, 128)
    Mc = ceil_div(M, 128)
    u = p.uid()
    wb = [p.sbuf(f"{tag}{u}_{i}", [128, Kc * 128], BF16) for i in range(2)]
    war = {0: [], 1: []}
    toks = []
    for mc in range(Mc):
        mrows = min(128, M - mc * 128)
        pp = mc % 2
        wtoks = []
        for kc in range(Kc):
            r = min(128, K - kc * 128)
            srcw = apv(w_dram, kc * 128 * Wcols + mc * 128, [[Wcols, r], [1, mrows]])
            wtoks.append(p.dma("gpsimd", wb[pp][0:r, kc * 128:kc * 128 + mrows], srcw,
                               wait=war[pp] if kc == 0 else ()))
        mm_toks = []
        for (s0, n) in ntiles(T, n_tile):
            bi, ps, w8 = pool.acquire()
            fns = []
            for kc in range(Kc):
                r = min(128, K - kc * 128)
                l = wb[pp][0:r, kc * 128:kc * 128 + mrows]
                rr = x_sb[0:r, kc * x_T + x_off + s0: kc * x_T + x_off + s0 + n]
                fns.append(lambda t, a=ps[0:mrows, 0:n], l=l, rr=rr,
                           st=(kc == 0), sp=(kc == Kc - 1): t.matmul(a, l, rr, start=st, stop=sp))
            tok_mm = p.ops("tensor", fns, wait=list(x_toks) + wtoks + w8)
            rd = out_cb(mc, mrows, s0, n, ps, tok_mm)
            pool.release(bi, rd)
            mm_toks.append(tok_mm)
            toks.append(rd)
        war[pp] = mm_toks
    return toks


def emit_proj_stream_ld(p, pool, w_dram, Wcols, x_get, x_T, K, M, T, out_cb,
                        n_tile=512, tag="wpl"):
    Kc = ceil_div(K, 128)
    Mc = ceil_div(M, 128)
    u = p.uid()
    wb = [p.sbuf(f"{tag}{u}_{i}", [128, Kc * 128], BF16) for i in range(2)]
    war = {0: [], 1: []}
    for mc in range(Mc):
        mrows = min(128, M - mc * 128)
        pp = mc % 2
        wtoks = []
        for kc in range(Kc):
            r = min(128, K - kc * 128)
            srcw = apv(w_dram, kc * 128 * Wcols + mc * 128, [[Wcols, r], [1, mrows]])
            wtoks.append(p.dma("gpsimd", wb[pp][0:r, kc * 128:kc * 128 + mrows], srcw,
                               wait=war[pp] if kc == 0 else ()))
        mm_toks = []
        for (s0, n) in ntiles(T, n_tile):
            bi, ps, w8 = pool.acquire()
            fns = []
            allw = []
            for kc in range(Kc):
                r = min(128, K - kc * 128)
                xap, xw = x_get(kc, s0, n)
                allw += xw
                l = wb[pp][0:r, kc * 128:kc * 128 + mrows]
                fns.append(lambda t, a=ps[0:mrows, 0:n], l=l, rr=xap,
                           st=(kc == 0), sp=(kc == Kc - 1): t.matmul(a, l, rr, start=st, stop=sp))
            tok_mm = p.ops("tensor", fns, wait=allw + wtoks + w8)
            for kc in range(Kc):
                x_get.note_reader(kc, s0, n, tok_mm)
            rd = out_cb(mc, mrows, s0, n, ps, tok_mm)
            pool.release(bi, rd)
            mm_toks.append(tok_mm)
    return



def build_l0_attn_a(T=2048, H=256, D=1024, dbg=False):
    Tall = T + H
    NKT = Tall // 128
    WLO, WHI = 1, 256
    offs, MW = attn_geometry(T, Tall, WLO, WHI, force_lo=H)
    p = Prog()
    pool = PsumPool(p, 6)
    c = p.consts()
    x_in = p.din("x_in", [D, Tall])
    w_qkv = p.din("w_qkv", [D, 3 * D])
    w_out = p.din("w_out", [D, D])
    mask_in = p.din("mask", [128, MW])
    y = p.dout("y", [D, T])

    x_get = make_x_loader(p, x_in, Tall, D, nbuf=10, eng="sync")
    xn = p.sbuf("xn", [128, 8 * Tall], BF16)
    xn_toks = emit_ln(p, pool, x_get, xn, Tall, D)

    mask_sb = p.sbuf("mask_sb", [128, MW], BF16)
    mask_tok = p.dma("gpsimd", mask_sb[:], mask_in.ap())

    o_fm = p.sbuf("o_fm", [128, 8 * T], BF16)
    pt_buf = p.sbuf("pt_buf", [128, NKT * 512], BF16)
    VSTRIDE = 192
    vtm = p.sbuf("vtm", [128, NKT * VSTRIDE], BF16)
    ones_v_tok = p.op("vector", lambda v: v.memset(
        sbv(vtm, NKT * VSTRIDE, 128, 64, [[VSTRIDE, NKT], [1, 64]]), 1.0))
    wq_sb = p.sbuf("wq_sb", [128, 8 * 128], BF16)
    wk_sb = p.sbuf("wk_sb", [128, 8 * 128], BF16)
    wv_sb = p.sbuf("wv_sb", [128, 8 * 128], BF16)
    qaug = [p.sbuf(f"qaug{i}", [65, T], BF16) for i in range(2)]
    kaug = [p.sbuf(f"kaug{i}", [65, Tall], BF16) for i in range(2)]
    sqh = p.sbuf("sqh", [64, Tall], BF16)
    nrow = p.sbuf("nrow", [1, Tall], F32)
    nrow2 = p.sbuf("nrow2", [1, T], F32)
    kmx = p.sbuf("kmx", [1, 2], F32)
    brow = p.sbuf("brow", [1, T], F32)
    rec_row = p.sbuf("rec_row", [1, 512], F32)
    rec_bf = p.sbuf("rec_bf", [1, 512], BF16)
    bc_o = p.sbuf("bc_o", [64, 512], BF16)

    epi = {"prev_t3": None}
    prev_pair_toks = []
    for hp in range(8):
        wtoks = []
        for kc in range(8):
            for (dst, base) in ((wq_sb, hp * 128), (wk_sb, 1024 + hp * 128),
                                (wv_sb, 2048 + hp * 128)):
                src2 = apv(w_qkv, kc * 128 * 3 * D + base, [[3 * D, 128], [1, 128]])
                wtoks.append(p.dma("gpsimd", dst[0:128, kc * 128:(kc + 1) * 128], src2,
                                   wait=prev_pair_toks if kc == 0 else ()))
        vtoks = [ones_v_tok]
        for tt in range(NKT):
            bi, ps, w8 = pool.acquire()
            fns = []
            for kc in range(8):
                l = xn[0:128, kc * Tall + tt * 128: kc * Tall + tt * 128 + 128]
                rr = wv_sb[0:128, kc * 128:(kc + 1) * 128]
                fns.append(lambda t, a=ps[0:128, 0:128], l=l, rr=rr,
                           st=(kc == 0), sp=(kc == 7): t.matmul(a, l, rr, start=st, stop=sp))
            tok_mm = p.ops("tensor", fns, wait=list(xn_toks) + wtoks + w8 + prev_pair_toks)
            dstv = sbv(vtm, NKT * VSTRIDE, 128, tt * VSTRIDE, [[128, 2], [1, 64]])
            tok_ev = p.op("vector", lambda v, o=dstv, i=ps[0:128, 0:128]:
                          v.tensor_copy(o, i), wait=[tok_mm])
            pool.release(bi, tok_ev)
            vtoks.append(tok_ev)
        qk_toks = {0: [], 1: []}
        for (wsb, dstbufs, t_off, t_len) in ((wq_sb, qaug, H, T), (wk_sb, kaug, 0, Tall)):
            for (s0, n) in ntiles(t_len, 512):
                bi, ps, w8 = pool.acquire()
                fns = []
                for kc in range(8):
                    l = wsb[0:128, kc * 128:(kc + 1) * 128]
                    rr = xn[0:128, kc * Tall + t_off + s0: kc * Tall + t_off + s0 + n]
                    fns.append(lambda t, a=ps[0:128, 0:n], l=l, rr=rr,
                               st=(kc == 0), sp=(kc == 7): t.matmul(a, l, rr, start=st, stop=sp))
                tok_mm = p.ops("tensor", fns, wait=list(xn_toks) + wtoks + w8 + prev_pair_toks)
                e0 = p.op("vector", lambda v, o=dstbufs[0][0:64, s0:s0 + n],
                          i=ps[0:64, 0:n]: v.tensor_copy(o, i),
                          wait=[tok_mm] + prev_pair_toks)
                e1 = p.op("scalar", lambda s, o=dstbufs[1][0:64, s0:s0 + n],
                          i=ps[64:128, 0:n]: s.activation(out=o, in_=i, func=AF.Copy),
                          wait=[tok_mm, e0] + prev_pair_toks)
                pool.release(bi, e1)
                qk_toks[0].append(e0)
                qk_toks[1].append(e1)
        ko_toks = [p.op("vector", lambda v, o=kaug[i][64:65, 0:Tall]: v.memset(o, 1.0),
                        wait=prev_pair_toks) for i in range(2)]
        pair_readers = []
        norm_tensor_toks = []
        head_scratch_war = []
        for hh in range(2):
            h = 2 * hp + hh
            qa, ka = qaug[hh], kaug[hh]
            t_sq = p.op("scalar", lambda s, o=sqh[0:64, 0:Tall], i=ka[0:64, 0:Tall]:
                        s.activation(out=o, in_=i, func=AF.Square),
                        wait=qk_toks[hh] + ko_toks + norm_tensor_toks)
            mm_toks = []
            for (s0, n) in ntiles(Tall, 512):
                bi, ps, w8 = pool.acquire()
                tok_mm = p.op("tensor", lambda t, a=ps[0:1, 0:n],
                              l=c["ones_col_bf"][0:64, 0:1], rr=sqh[0:64, s0:s0 + n]:
                              t.matmul(a, l, rr, start=True, stop=True),
                              wait=[t_sq] + w8 + c["tok"])
                te = p.op("vector", lambda v, o=nrow[0:1, s0:s0 + n], i=ps[0:1, 0:n]:
                          v.tensor_copy(o, i), wait=[tok_mm])
                pool.release(bi, te)
                mm_toks.append(te)
                norm_tensor_toks.append(tok_mm)
            t_mx = p.op("vector", lambda v, o=kmx[0:1, 0:1], i=nrow[0:1, 0:Tall]:
                        v.tensor_reduce(o, i, axis=AX.X, op=ALU.max), wait=mm_toks)
            t_sq2 = p.op("scalar", lambda s, o=sqh[0:64, 0:T], i=qa[0:64, 0:T]:
                         s.activation(out=o, in_=i, func=AF.Square),
                         wait=qk_toks[hh] + norm_tensor_toks)
            mm2 = []
            for (s0, n) in ntiles(T, 512):
                bi, ps, w8 = pool.acquire()
                tok_mm = p.op("tensor", lambda t, a=ps[0:1, 0:n],
                              l=c["ones_col_bf"][0:64, 0:1], rr=sqh[0:64, s0:s0 + n]:
                              t.matmul(a, l, rr, start=True, stop=True),
                              wait=[t_sq2] + w8)
                te = p.op("vector", lambda v, o=nrow2[0:1, s0:s0 + n], i=ps[0:1, 0:n]:
                          v.tensor_copy(o, i), wait=[tok_mm])
                pool.release(bi, te)
                mm2.append(te)
                norm_tensor_toks.append(tok_mm)
            tb1 = p.op("vector", lambda v, o=brow[0:1, 0:T], i=nrow2[0:1, 0:T],
                       s1=kmx[0:1, 0:1]: v.tensor_single_scalar(o, i, s1, op=ALU.mult),
                       wait=mm2 + [t_mx])
            tb2 = p.op("scalar", lambda s, o=brow[0:1, 0:T], i=brow[0:1, 0:T]:
                       s.activation(out=o, in_=i, func=AF.Sqrt), wait=[tb1])
            tb3 = p.op("vector", lambda v, o=qa[64:65, 0:T], i=brow[0:1, 0:T]:
                       v.tensor_scalar_mul(o, i, -1.0), wait=[tb2])

            srow = 64 if hh == 0 else 0
            obase = 0 if hh == 0 else 64

            def o_cb(c0, n, psB, tok, hh=hh, hp=hp, srow=srow, obase=obase):
                war3 = [epi["prev_t3"]] if epi["prev_t3"] else []
                t1 = p.op("vector", lambda v, o=rec_row[0:1, 0:n],
                          i=psB[srow:srow + 1, 0:n]:
                          v.reciprocal(o, i), wait=[tok])
                t2 = p.op("vector", lambda v, o=rec_bf[0:1, 0:n], i=rec_row[0:1, 0:n]:
                          v.tensor_copy(o, i), wait=[t1] + war3)
                bi2, psC, w82 = pool.acquire()
                t3 = p.op("tensor", lambda t, a=psC[0:64, 0:n],
                          l=c["ones_row_bf"][0:1, 0:64], rr=rec_bf[0:1, 0:n]:
                          t.matmul(a, l, rr, start=True, stop=True), wait=[t2] + w82)
                epi["prev_t3"] = t3
                t4 = p.op("vector", lambda v, o=bc_o[0:64, 0:n], i=psC[0:64, 0:n]:
                          v.tensor_copy(o, i), wait=[t3])
                pool.release(bi2, t4)
                t5 = p.op("vector", lambda v, o=o_fm[hh * 64:hh * 64 + 64,
                                                     hp * T + c0:hp * T + c0 + n],
                          i=psB[obase:obase + 64, 0:n], j=bc_o[0:64, 0:n]:
                          v.tensor_mul(o, i, j),
                          wait=[t4])
                return t5
            rd = emit_attn_head(
                p, pool, kaug_h=ka, k_toks=qk_toks[hh] + ko_toks,
                qaug_h=qa, q_toks=qk_toks[hh] + [tb3], Tq=T, Tkv=Tall,
                wlo=WLO, whi=WHI, scale=0.125,
                mask_sb=mask_sb, mask_offs=offs, mask_toks=[mask_tok],
                vtm=vtm, vt_stride=VSTRIDE, vt_total=NKT * VSTRIDE,
                v_off=hh * 64,
                v_toks=vtoks, pt_buf=pt_buf, pt_stride=512, o_cb=o_cb,
                scratch_war=head_scratch_war)
            head_scratch_war = rd
            pair_readers += rd
        prev_pair_toks = pair_readers + [t_sq, t_sq2] + norm_tensor_toks[-1:]

    if dbg:
        d_xn = p.dout("d_xn", [128, 8 * Tall])
        p.dma("gpsimd", d_xn.ap(), xn[:], wait=list(xn_toks), store=True)
        d_o = p.dout("d_o", [128, 8 * T])
        p.dma("gpsimd", d_o.ap(), o_fm[:], wait=prev_pair_toks, store=True)
        d_vt = p.dout("d_vt", [128, NKT * VSTRIDE])
        p.dma("gpsimd", d_vt.ap(), vtm[:], wait=prev_pair_toks, store=True)
        d_pt = p.dout("d_pt", [128, NKT * 512])
        p.dma("gpsimd", d_pt.ap(), pt_buf[:], wait=prev_pair_toks, store=True)
        d_qa = p.dout("d_qa", [65, T])
        p.dma("gpsimd", d_qa.ap(), qaug[1][0:65, 0:T], wait=prev_pair_toks, store=True)
        d_ka = p.dout("d_ka", [65, Tall])
        p.dma("gpsimd", d_ka.ap(), kaug[1][0:65, 0:Tall], wait=prev_pair_toks, store=True)

    y_sb = [p.sbuf(f"ysb{i}", [128, 512], F32) for i in range(4)]
    y_war = {}
    o_all_toks = prev_pair_toks
    ctr = {"i": 0}

    def out_cb(mc, mrows, s0, n, ps, tok_mm):
        xi = ctr["i"] % 4
        ctr["i"] += 1
        xap, xw = x_get(mc, H + s0, n)
        war = y_war.pop(xi, [])
        t1 = p.op("vector", lambda v, o=y_sb[xi][0:mrows, 0:n], i=ps[0:mrows, 0:n],
                  j=xap: v.tensor_add(o, i, j), wait=[tok_mm] + xw + war)
        x_get.note_reader(mc, H + s0, n, t1)
        dst = apv(y, mc * 128 * T + s0, [[T, mrows], [1, n]])
        td = p.dma("sync", dst, y_sb[xi][0:mrows, 0:n], wait=[t1], store=True)
        y_war[xi] = [td]
        return t1

    emit_proj_stream(p, pool, w_out, D, o_fm, 0, T, o_all_toks, D, D, T, out_cb)
    p.finish()
    return p.build()


def l0_attn_a_host_mask(T=2048, H=256, qc_is_zero=False):
    return attn_mask_host(T, T + H, 1, 256,
                          valid_kv_lo=(H if qc_is_zero else 0), force_lo=H)



def build_l0_ffn_a(T=2048, HQ=128, H=768, D=1024, dbg=False, ln=True):
    Tq = T + HQ
    Tall = T + H
    NKT = Tall // 128
    WLO, WHI = H - HQ - 512, H - HQ
    WIN = WHI - WLO + 1
    INNER = 512
    offs, MW = attn_geometry(Tq, Tall, WLO, WHI, force_lo=H)
    p = Prog()
    pool = PsumPool(p, 4)
    wide = PsumPool(p, 2, name="wp", width=1024)
    c = p.consts()
    x_in = p.din("x_in", [D, Tall])
    wqkv = p.din("wqkv", [D, 1536])
    wqksw = p.din("wqksw", [D, 1024])
    wo = p.din("wo", [INNER, D])
    qs_in = p.din("qs_col", [128, 2])
    ks_in = p.din("ks_col", [128, 2])
    csq_in = p.din("csq", [128, Tq])
    ssq_in = p.din("ssq", [128, Tq])
    csk_in = p.din("csk", [128, Tall])
    ssk_in = p.din("ssk", [128, Tall])
    mask_in = p.din("mask", [128, MW])
    id8_in = p.din("id8", [8, 8])
    e8row_in = p.din("e8row", [1, 64])
    qsr_in = p.din("qs_row", [1, 64])
    ksr_in = p.din("ks_row", [1, 64])
    xmid = p.dout("xmid", [D, Tq])

    x_get = make_x_loader(p, x_in, Tall, D, nbuf=10, eng="sync")
    xn = p.sbuf("xn", [128, 8 * Tall], BF16)
    if ln:
        xn_toks = emit_ln(p, pool, x_get, xn, Tall, D)
    else:
        xn_toks = []
        for (s0, n) in ntiles(Tall, 512):
            for ci in range(8):
                xap, xw = x_get(ci, s0, n)
                tcp = p.op("vector",
                           lambda v, o=xn[0:128, ci * Tall + s0:ci * Tall + s0 + n],
                           i=xap: v.tensor_copy(o, i), wait=list(xw))
                x_get.note_reader(ci, s0, n, tcp)
                xn_toks.append(tcp)

    def ldcast(name, srct, shape):
        t = p.sbuf(name, shape, BF16)
        return t, p.dma("gpsimd", t[:], srct.ap())
    mask_sb = p.sbuf("mask_sb", [128, MW], BF16)
    mask_tok = p.dma("gpsimd", mask_sb[:], mask_in.ap())
    csq, t1t = ldcast("csq_sb", csq_in, [128, Tq])
    ssq, t2t = ldcast("ssq_sb", ssq_in, [128, Tq])
    csk, t3t = ldcast("csk_sb", csk_in, [128, Tall])
    ssk, t4t = ldcast("ssk_sb", ssk_in, [128, Tall])
    tab_toks = [t1t, t2t, t3t, t4t]
    qs_col = p.sbuf("qs_col_sb", [128, 2], F32)
    ks_col = p.sbuf("ks_col_sb", [128, 2], F32)
    e8row = p.sbuf("e8row_sb", [1, 64], BF16)
    id8f = p.sbuf("id8f_sb", [8, 8], BF16)
    col_toks = [p.dma("gpsimd", qs_col[:], qs_in.ap()),
                p.dma("gpsimd", ks_col[:], ks_in.ap()),
                p.dma("gpsimd", e8row[:], e8row_in.ap()),
                p.dma("gpsimd", id8f[:], id8_in.ap())]

    o_fm = p.sbuf("o_fm", [128, 4 * Tq], BF16)
    PTS = 640
    ptb = p.sbuf("ptb", [128, NKT * PTS], BF16)
    _dummy = None
    VSTRIDE = 192
    vtm = p.sbuf("vtm", [128, NKT * VSTRIDE], BF16)
    ones_v_tok = p.op("vector", lambda v: v.memset(
        sbv(vtm, NKT * VSTRIDE, 128, 64, [[VSTRIDE, NKT], [1, 64]]), 1.0))
    wq_sb = p.sbuf("wq_sb", [128, 8 * 128], BF16)
    wk_sb = p.sbuf("wk_sb", [128, 8 * 128], BF16)
    wv_sb = p.sbuf("wv_sb", [128, 8 * 128], BF16)
    wqs_sb = p.sbuf("wqs_sb", [128, 8 * 128], BF16)
    wks_sb = p.sbuf("wks_sb", [128, 8 * 128], BF16)
    q_raw = p.sbuf("q_raw", [128, Tq], BF16)
    qsw_raw = p.sbuf("qsw_raw", [128, Tq], BF16)
    k_raw = p.sbuf("k_raw", [128, Tall], BF16)
    ksw_raw = p.sbuf("ksw_raw", [128, Tall], BF16)
    tmpA = p.sbuf("tmpA", [128, Tall], BF16)
    tmpB = p.sbuf("tmpB", [128, Tall], BF16)
    qaug = [p.sbuf(f"qaug{i}", [65, Tq], BF16) for i in range(2)]
    kaug = [p.sbuf(f"kaug{i}", [65, Tall], BF16) for i in range(2)]
    sqsq = p.sbuf("sqsq", [128, Tq], BF16)
    sqsk = p.sbuf("sqsk", [128, Tall], BF16)
    nrowq = p.sbuf("nrowq", [1, Tq], F32)
    nrowk = p.sbuf("nrowk", [1, Tall], F32)
    rbq = p.sbuf("rbq", [128, Tq], BF16)
    rbk = p.sbuf("rbk", [128, Tall], BF16)
    rec_row = p.sbuf("rec_row", [1, 512], F32)
    rec_bf = p.sbuf("rec_bf", [1, 512], BF16)
    bc_o = p.sbuf("bc_o", [64, 512], BF16)

    gsc = p.sbuf("gsc", [1, 132], F32)
    g_ld = [p.dma("gpsimd", gsc[0:1, 0:64], qsr_in.ap()),
            p.dma("gpsimd", gsc[0:1, 64:128], ksr_in.ap())]
    t_g0 = p.ops("vector", [
        lambda v: v.tensor_reduce(gsc[0:1, 128:129], gsc[0:1, 0:64],
                                  axis=AX.X, op=ALU.max),
        lambda v: v.tensor_reduce(gsc[0:1, 129:130], gsc[0:1, 64:128],
                                  axis=AX.X, op=ALU.max),
        lambda v: v.tensor_mul(gsc[0:1, 128:129], gsc[0:1, 128:129],
                               gsc[0:1, 129:130]),
        lambda v: v.tensor_scalar_mul(gsc[0:1, 128:129], gsc[0:1, 128:129], -1.0),
    ], wait=g_ld, chain=True)
    gcol = p.sbuf("gcol", [128, 1], F32)
    big_, psg_, w8g_ = pool.acquire()
    t_gb = p.op("tensor", lambda t: t.matmul(psg_[0:128, 0:1],
                                             c["ones_row_f32"][0:1, 0:128],
                                             gsc[0:1, 128:129],
                                             start=True, stop=True),
                wait=[t_g0] + w8g_ + c["tok"])
    t_g0 = p.op("vector", lambda v: v.tensor_copy(gcol[:], psg_[0:128, 0:1]),
                wait=[t_gb])
    pool.release(big_, t_g0)
    gbc = gcol[64:65, 0:1].copy()
    gbc.ap = vp([[1, 1], [0, Tq]])

    def shifted_rr(kc, base, s0, n):
        off = kc * Tall + base + s0 - (1 if kc >= 4 else 0)
        return xn[0:128, off:off + n]

    epi = {"prev_t3": None}
    p._ffn_sq_war = []
    prev_pair = []
    for hp in range(4):
        wtoks = []
        for kc in range(8):
            for (dst, wsrc, W, base) in ((wq_sb, wqkv, 1536, hp * 128),
                                         (wk_sb, wqkv, 1536, 512 + hp * 128),
                                         (wv_sb, wqkv, 1536, 1024 + hp * 128),
                                         (wqs_sb, wqksw, 1024, hp * 128),
                                         (wks_sb, wqksw, 1024, 512 + hp * 128)):
                src2 = apv(wsrc, kc * 128 * W + base, [[W, 128], [1, 128]])
                wtoks.append(p.dma("gpsimd", dst[0:128, kc * 128:(kc + 1) * 128], src2,
                                   wait=prev_pair if kc == 0 else ()))
        vtoks = [ones_v_tok]
        for tt in range(1, NKT):
            bi, ps, w8 = pool.acquire()
            fns = []
            for kc in range(8):
                l = shifted_rr(kc, 0, tt * 128, 128)
                rr = wv_sb[0:128, kc * 128:(kc + 1) * 128]
                fns.append(lambda t, a=ps[0:128, 0:128], l=l, rr=rr,
                           st=(kc == 0), sp=(kc == 7): t.matmul(a, l, rr, start=st, stop=sp))
            tok_mm = p.ops("tensor", fns, wait=list(xn_toks) + wtoks + w8 + prev_pair)
            dstv = sbv(vtm, NKT * VSTRIDE, 128, tt * VSTRIDE, [[128, 2], [1, 64]])
            tok_ev = p.op("vector", lambda v, o=dstv, i=ps[0:128, 0:128]:
                          v.tensor_copy(o, i), wait=[tok_mm])
            pool.release(bi, tok_ev)
            vtoks.append(tok_ev)
        raw_toks = []
        for (wsb, dst, base, tlen, t0) in ((wq_sb, q_raw, H - HQ, Tq, 0),
                                           (wqs_sb, qsw_raw, H - HQ, Tq, 0),
                                           (wk_sb, k_raw, 0, Tall, 128),
                                           (wks_sb, ksw_raw, 0, Tall, 128)):
            for (s0, n) in ntiles(tlen - t0, 512):
                bi, ps, w8 = pool.acquire()
                fns = []
                for kc in range(8):
                    l = wsb[0:128, kc * 128:(kc + 1) * 128]
                    rr = shifted_rr(kc, base, t0 + s0, n)
                    fns.append(lambda t, a=ps[0:128, 0:n], l=l, rr=rr,
                               st=(kc == 0), sp=(kc == 7): t.matmul(a, l, rr, start=st, stop=sp))
                tok_mm = p.ops("tensor", fns, wait=list(xn_toks) + wtoks + w8 + prev_pair)
                te = p.op("vector", lambda v, o=dst[0:128, t0 + s0:t0 + s0 + n],
                          i=ps[0:128, 0:n]: v.tensor_copy(o, i), wait=[tok_mm] + prev_pair)
                pool.release(bi, te)
                raw_toks.append(te)

        def transform(raw, raw_sw, sq, tabc, tabs, scol, nrow, rb, out0, out1,
                      t0, tlen, war=()):
            tsq = p.op("scalar", lambda s, o=sq[0:128, t0:tlen], i=raw[0:128, t0:tlen]:
                       s.activation(out=o, in_=i, func=AF.Square),
                       wait=raw_toks + prev_pair + list(war) + p._ffn_sq_war)
            p._ffn_sq_war = []
            war_rb = list(war)
            rb_toks = []
            for half in range(2):
                bp = half * 64
                for (s0, n) in ntiles(tlen - t0, 512):
                    bi, ps, w8 = pool.acquire()
                    t_mm = p.op("tensor", lambda t, a=ps[0:1, 0:n],
                                l=c["ones_col_bf"][bp:bp + 64, 0:1],
                                rr=sq[bp:bp + 64, t0 + s0:t0 + s0 + n]:
                                t.matmul(a, l, rr, start=True, stop=True),
                                wait=[tsq] + w8 + c["tok"])
                    p._ffn_sq_war.append(t_mm)
                    t1_ = p.op("scalar", lambda s, o=nrow[0:1, s0:s0 + n], i=ps[0:1, 0:n]:
                               s.activation(out=o, in_=i, func=AF.Sqrt), wait=[t_mm])
                    t2a = p.ops("vector", [
                        lambda v, o=nrow[0:1, s0:s0 + n], i=nrow[0:1, s0:s0 + n]:
                        v.tensor_scalar_max(o, i, 1e-12),
                        lambda v, o=nrow[0:1, s0:s0 + n], i=nrow[0:1, s0:s0 + n]:
                        v.reciprocal(o, i),
                    ], wait=[t1_], chain=True)
                    war3 = [epi["prev_t3"]] if epi["prev_t3"] else []
                    t2_ = p.op("vector", lambda v, o=rec_bf[0:1, 0:n],
                               i=nrow[0:1, s0:s0 + n]: v.tensor_copy(o, i),
                               wait=[t2a] + war3)
                    pool.release(bi, t2_)
                    bi2, ps2, w82 = pool.acquire()
                    t4_ = p.op("tensor", lambda t, a=ps2[bp:bp + 64, 0:n],
                               l=c["ones_row_bf"][0:1, 0:64], rr=rec_bf[0:1, 0:n]:
                               t.matmul(a, l, rr, start=True, stop=True),
                               wait=[t2_] + w82)
                    epi["prev_t3"] = t4_
                    t5_ = p.op("vector", lambda v, o=rb[bp:bp + 64, t0 + s0:t0 + s0 + n],
                               i=ps2[bp:bp + 64, 0:n]: v.tensor_copy(o, i),
                               wait=[t4_] + war_rb)
                    pool.release(bi2, t5_)
                    rb_toks.append(t5_)
            ta = p.op("vector", lambda v, o=tmpA[0:128, t0:tlen], i=raw[0:128, t0:tlen],
                      s1=scol[0:128, 0:1], j=rb[0:128, t0:tlen]:
                      v.scalar_tensor_tensor(o, i, s1, j, op0=ALU.mult, op1=ALU.mult),
                      wait=rb_toks + col_toks + raw_toks)
            tb = p.op("vector", lambda v, o=tmpB[0:128, t0:tlen], i=raw_sw[0:128, t0:tlen],
                      s1=scol[0:128, 1:2], j=rb[0:128, t0:tlen]:
                      v.scalar_tensor_tensor(o, i, s1, j, op0=ALU.mult, op1=ALU.mult),
                      wait=rb_toks + col_toks + raw_toks)
            tc = p.op("vector", lambda v, o=raw[0:128, t0:tlen], i=tmpA[0:128, t0:tlen],
                      j=tabc[0:128, t0:tlen]: v.tensor_mul(o, i, j),
                      wait=[ta, tb] + tab_toks)
            td = p.op("vector", lambda v, o=tmpB[0:128, t0:tlen], i=tmpB[0:128, t0:tlen],
                      j=tabs[0:128, t0:tlen]: v.tensor_mul(o, i, j), wait=[tb] + tab_toks)
            te0 = p.op("vector", lambda v, o=out0[0:64, t0:tlen], i=raw[0:64, t0:tlen],
                       j=tmpB[0:64, t0:tlen]: v.tensor_add(o, i, j),
                       wait=[tc, td] + prev_pair)
            te1 = p.op("vector", lambda v, o=out1[0:64, t0:tlen], i=raw[64:128, t0:tlen],
                       j=tmpB[64:128, t0:tlen]: v.tensor_add(o, i, j),
                       wait=[tc, td] + prev_pair)
            return [te0, te1]
        qtr = transform(q_raw, qsw_raw, sqsq, csq, ssq, qs_col, nrowq, rbq,
                        qaug[0], qaug[1], 0, Tq)
        ktr = transform(k_raw, ksw_raw, sqsk, csk, ssk, ks_col, nrowk, rbk,
                        kaug[0], kaug[1], 128, Tall, war=qtr)
        ko_toks = [p.op("vector", lambda v, o=kaug[i][64:65, 0:Tall]: v.memset(o, 1.0),
                        wait=prev_pair) for i in range(2)]
        pair_readers = []
        head_war = []
        for hh in range(2):
            h = 2 * hp + hh
            qa, ka = qaug[hh], kaug[hh]
            tb3 = p.ops("vector", [
                lambda v, o=qa[64:65, 0:Tq]: v.memset(o, 0.0),
                lambda v, o=qa[64:65, 0:Tq], j=gbc: v.tensor_add(o, o, j),
            ], wait=[t_g0] + qtr + prev_pair, chain=True)
            srow = 64 if hh == 0 else 0
            obase = 0 if hh == 0 else 64

            def o_cb(c0, n, psB, tok, hh=hh, hp=hp, srow=srow, obase=obase):
                war3 = [epi["prev_t3"]] if epi["prev_t3"] else []
                t1 = p.ops("vector", [
                    lambda v, o=rec_row[0:1, 0:n], i=psB[srow:srow + 1, 0:n]:
                    v.tensor_scalar_max(o, i, 1e-30),
                    lambda v, o=rec_row[0:1, 0:n], i=rec_row[0:1, 0:n]:
                    v.reciprocal(o, i),
                ], wait=[tok], chain=True)
                t2 = p.op("vector", lambda v, o=rec_bf[0:1, 0:n], i=rec_row[0:1, 0:n]:
                          v.tensor_copy(o, i), wait=[t1] + war3)
                bi2, psC, w82 = pool.acquire()
                t3 = p.op("tensor", lambda t, a=psC[0:64, 0:n],
                          l=c["ones_row_bf"][0:1, 0:64], rr=rec_bf[0:1, 0:n]:
                          t.matmul(a, l, rr, start=True, stop=True), wait=[t2] + w82)
                epi["prev_t3"] = t3
                t4 = p.op("vector", lambda v, o=bc_o[0:64, 0:n], i=psC[0:64, 0:n]:
                          v.tensor_copy(o, i), wait=[t3])
                pool.release(bi2, t4)
                t5 = p.op("vector", lambda v, o=o_fm[hh * 64:hh * 64 + 64,
                                                     hp * Tq + c0:hp * Tq + c0 + n],
                          i=psB[obase:obase + 64, 0:n], j=bc_o[0:64, 0:n]:
                          v.tensor_mul(o, i, j), wait=[t4])
                return t5
            rd = emit_attn_head(
                p, pool, kaug_h=ka, k_toks=ktr + ko_toks,
                qaug_h=qa, q_toks=qtr + [tb3], Tq=Tq, Tkv=Tall,
                wlo=WLO, whi=WHI, scale=8.0,
                mask_sb=mask_sb, mask_offs=offs, mask_toks=[mask_tok],
                vtm=vtm, vt_stride=VSTRIDE, vt_total=NKT * VSTRIDE,
                v_off=hh * 64, v_toks=vtoks, pt_buf=ptb, pt_stride=PTS,
                o_cb=o_cb, scratch_war=head_war, score_pool=wide,
                mask_mode="add_pre")
            head_war = rd
            pair_readers += rd
        prev_pair = pair_readers + qtr + ktr

    if dbg:
        allw = prev_pair
        for (nm, tt, sh) in (("d_qa0", qaug[0], [65, Tq]), ("d_qa1", qaug[1], [65, Tq]),
                             ("d_ka0", kaug[0], [65, Tall]), ("d_ka1", kaug[1], [65, Tall]),
                             ("d_ptb", ptb, [128, NKT * PTS]),
                             ("d_xn", xn, [128, 8 * Tall]),
                             ("d_ofm", o_fm, [128, 4 * Tq])):
            dt_ = p.dout(nm, sh, BF16)
            p.dma("gpsimd", dt_.ap(), tt[0:sh[0], 0:sh[1]], wait=allw, store=True)

    y_sb = [p.sbuf(f"ysb{i}", [128, 512], F32) for i in range(1)]
    y_war = {}
    ctr = {"i": 0}

    def out_cb(mc, mrows, s0, n, ps, tok_mm):
        xi = 0
        ctr["i"] += 1
        xap = xn[0:mrows, mc * Tall + (H - HQ) + s0:mc * Tall + (H - HQ) + s0 + n]
        war = y_war.pop(xi, [])
        t1 = p.op("vector", lambda v, o=y_sb[xi][0:mrows, 0:n], i=ps[0:mrows, 0:n],
                  j=xap: v.tensor_add(o, i, j), wait=[tok_mm] + list(xn_toks) + war)
        dst = apv(xmid, mc * 128 * Tq + s0, [[Tq, mrows], [1, n]])
        td = p.dma("sync", dst, y_sb[xi][0:mrows, 0:n], wait=[t1], store=True)
        y_war[xi] = [td]
        return t1

    emit_proj_stream(p, pool, wo, D, o_fm, 0, Tq, prev_pair, INNER, D, Tq, out_cb)
    p.finish()
    return p.build()



def build_l0_ffn_b(T=2048, D=1024, HID=2730):
    MHC = ceil_div(HID, 128)
    p = Prog()
    pool = PsumPool(p, 6)
    c = p.consts()
    xm_in = p.din("xm_in", [D, 1 + T])
    base_in = p.din("base_in", [D, T])
    lg_in = p.din("lg_col", [128, 8])
    lb_in = p.din("lb_col", [128, 8])
    w1 = p.din("w1", [D, 2 * HID])
    b1a_in = p.din("b1a_col", [128, MHC])
    b1g_in = p.din("b1g_col", [128, MHC])
    w2 = p.din("w2", [HID, D])
    b2_in = p.din("b2_col", [128, 8])
    y = p.dout("y", [D, T])

    lg = p.sbuf("lg_sb", [128, 8], F32)
    lb = p.sbuf("lb_sb", [128, 8], F32)
    b1a = p.sbuf("b1a_sb", [128, MHC], F32)
    b1g = p.sbuf("b1g_sb", [128, MHC], F32)
    b2 = p.sbuf("b2_sb", [128, 8], F32)
    cst = [p.dma("gpsimd", lg[:], lg_in.ap()), p.dma("gpsimd", lb[:], lb_in.ap()),
           p.dma("gpsimd", b1a[:], b1a_in.ap()), p.dma("gpsimd", b1g[:], b1g_in.ap()),
           p.dma("gpsimd", b2[:], b2_in.ap())]

    bufs = [p.sbuf(f"xl{i}", [128, 512], F32) for i in range(10)]
    state = {"i": 0, "cache": {}, "readers": {}}

    def x_get(ci, s0, n):
        key = (ci, s0, n)
        if key in state["cache"]:
            b_i, tok = state["cache"][key]
            return bufs[b_i][0:128, 0:n], [tok]
        b_i = state["i"]
        state["i"] = (state["i"] + 1) % 10
        col = s0 + (0 if ci >= 4 else 1)
        src = apv(xm_in, ci * 128 * (1 + T) + col, [[1 + T, 128], [1, n]])
        war = state["readers"].pop(b_i, [])
        tok = p.dma("sync", bufs[b_i][0:128, 0:n], src, wait=war)
        state["cache"] = {k: v for k, v in state["cache"].items() if v[0] != b_i}
        state["cache"][key] = (b_i, tok)
        return bufs[b_i][0:128, 0:n], [tok]

    def note_reader(ci, s0, n, tok):
        if tok is None:
            return
        key = (ci, s0, n)
        if key in state["cache"]:
            b_i, _ = state["cache"][key]
            state["readers"].setdefault(b_i, []).append(tok)
    x_get.note_reader = note_reader

    xn = p.sbuf("xn2", [128, 8 * T], BF16)
    xn_toks = emit_ln(p, pool, x_get, xn, T, D, g_col=lg, b_col=lb, aff_toks=cst)

    w2_sb, w2_toks = load_weight(p, "w2_sb", w2, HID, D)
    w1a = [p.sbuf(f"w1a{i}", [128, 8 * 128], BF16) for i in range(2)]
    w1g = [p.sbuf(f"w1g{i}", [128, 8 * 128], BF16) for i in range(2)]
    h4 = p.sbuf("h4", [128, MHC * 512], BF16)
    gl = p.sbuf("gl", [128, 512], BF16)
    y_sb = [p.sbuf(f"ysb{i}", [128, 512], F32) for i in range(4)]
    bb = [p.sbuf(f"bb{i}", [128, 512], F32) for i in range(2)]
    y_war = {}
    b_war = {}
    w1_war = {0: [], 1: []}
    ctr = {"i": 0}

    prev_out = []
    for (s0, n) in ntiles(T, 512):
        h4_toks = []
        for m in range(MHC):
            mrows = min(128, HID - m * 128)
            pp = m % 2
            wta = []
            wtg = []
            for kc in range(8):
                srca = apv(w1, kc * 128 * 2 * HID + m * 128, [[2 * HID, 128], [1, mrows]])
                wta.append(p.dma("gpsimd", w1a[pp][0:128, kc * 128:kc * 128 + mrows],
                                 srca, wait=w1_war[pp] if kc == 0 else ()))
                srcg = apv(w1, kc * 128 * 2 * HID + HID + m * 128, [[2 * HID, 128], [1, mrows]])
                wtg.append(p.dma("gpsimd", w1g[pp][0:128, kc * 128:kc * 128 + mrows],
                                 srcg, wait=()))
            bia, psa, w8a = pool.acquire()
            big, psg, w8g = pool.acquire()
            fa = []
            fg = []
            for kc in range(8):
                rr = xn[0:128, kc * T + s0:kc * T + s0 + n]
                fa.append(lambda t, a=psa[0:mrows, 0:n], l=w1a[pp][0:128, kc * 128:kc * 128 + mrows],
                          rr=rr, st=(kc == 0), sp=(kc == 7): t.matmul(a, l, rr, start=st, stop=sp))
                fg.append(lambda t, a=psg[0:mrows, 0:n], l=w1g[pp][0:128, kc * 128:kc * 128 + mrows],
                          rr=rr, st=(kc == 0), sp=(kc == 7): t.matmul(a, l, rr, start=st, stop=sp))
            tok_a = p.ops("tensor", fa, wait=list(xn_toks) + wta + w8a)
            tok_g = p.ops("tensor", fg, wait=wtg + w8g)
            w1_war[pp] = [tok_a, tok_g]
            t_gl = p.op("scalar", lambda s, o=gl[0:mrows, 0:n], i=psg[0:mrows, 0:n],
                        b=b1g[0:mrows, m:m + 1]:
                        s.activation(out=o, in_=i, func=AF.Gelu, bias=b),
                        wait=[tok_g] + cst + (h4_toks[-1:] if h4_toks else prev_out))
            t_h4 = p.op("vector", lambda v, o=h4[0:mrows, m * 512:m * 512 + n],
                        i=psa[0:mrows, 0:n], s1=b1a[0:mrows, m:m + 1], j=gl[0:mrows, 0:n]:
                        v.scalar_tensor_tensor(o, i, s1, j, op0=ALU.add, op1=ALU.mult),
                        wait=[tok_a, t_gl] + cst + prev_out)
            pool.release(bia, t_h4)
            pool.release(big, t_h4)
            h4_toks.append(t_h4)
        for mc in range(8):
            bi, ps, w8 = pool.acquire()
            fns = []
            for k in range(MHC):
                krows = min(128, HID - k * 128)
                l = w2_sb[0:krows, k * D + mc * 128:k * D + mc * 128 + 128]
                rr = h4[0:krows, k * 512:k * 512 + n]
                fns.append(lambda t, a=ps[0:128, 0:n], l=l, rr=rr,
                           st=(k == 0), sp=(k == MHC - 1): t.matmul(a, l, rr, start=st, stop=sp))
            tok_mm = p.ops("tensor", fns, wait=h4_toks + w2_toks + w8)
            xi = ctr["i"] % 4
            bxi = ctr["i"] % 2
            ctr["i"] += 1
            xr = y_sb[xi]
            br = bb[bxi]
            war = y_war.pop(xi, [])
            bwar = b_war.pop(bxi, [])
            srcx = apv(xm_in, mc * 128 * (1 + T) + 1 + s0, [[1 + T, 128], [1, n]])
            t_x = p.dma("sync", xr[0:128, 0:n], srcx, wait=war)
            srcb = apv(base_in, mc * 128 * T + s0, [[T, 128], [1, n]])
            t_b = p.dma("sync", br[0:128, 0:n], srcb, wait=bwar)
            t1 = p.op("vector", lambda v, o=xr[0:128, 0:n], i=ps[0:128, 0:n],
                      s1=b2[0:128, mc:mc + 1], j=xr[0:128, 0:n]:
                      v.scalar_tensor_tensor(o, i, s1, j, op0=ALU.add, op1=ALU.add),
                      wait=[tok_mm, t_x] + cst)
            t2 = p.op("vector", lambda v, o=xr[0:128, 0:n], i=xr[0:128, 0:n],
                      j=br[0:128, 0:n]: v.tensor_add(o, i, j), wait=[t1, t_b])
            pool.release(bi, t2)
            dst = apv(y, mc * 128 * T + s0, [[T, 128], [1, n]])
            td = p.dma("sync", dst, xr[0:128, 0:n], wait=[t2], store=True)
            y_war[xi] = [td]
            b_war[bxi] = [t2]
            prev_out.append(t2)
        prev_out = prev_out[-8:]
    p.finish()
    return p.build()



def l0_ffn_a_host(wqkv_l, qs_l, ks_l, L, start, T=2048, HQ=128, H=768,
                  WINDOW=512, DIM_HEAD=64):
    D = wqkv_l.shape[0]
    Tq, Tall = T + HQ, T + H
    perm = np.concatenate([(np.arange(64) + 32) % 64 + 64 * h for h in range(8)])
    wqksw = np.concatenate([wqkv_l[:, perm], wqkv_l[:, 512 + perm]], axis=1)
    qs_col = np.stack([np.tile(qs_l, 2), np.tile(qs_l[(np.arange(64) + 32) % 64], 2)],
                      axis=1).astype(np.float32)
    ks_col = np.stack([np.tile(ks_l, 2), np.tile(ks_l[(np.arange(64) + 32) % 64], 2)],
                      axis=1).astype(np.float32)
    half_idx = np.arange(0, DIM_HEAD, 2, dtype=np.float32)
    inv_freq = 1.0 / 10000.0 ** (half_idx / DIM_HEAD)
    base = (half_idx + 0.4 * DIM_HEAD) / (1.4 * DIM_HEAD)
    sign = np.concatenate([-np.ones(32, np.float32), np.ones(32, np.float32)])

    def tables(pos, inv_scale):
        pos = np.clip(pos, 0, L - 1).astype(np.float32)
        freqs = pos[None, :] * np.concatenate([inv_freq, inv_freq])[:, None]
        power = (pos - L // 2) / float(WINDOW // 2)
        scl = np.concatenate([base, base])[:, None] ** power[None, :]
        if inv_scale:
            scl = 1.0 / scl
        cs = np.cos(freqs) * scl
        ss = sign[:, None] * np.sin(freqs) * scl
        return (np.tile(cs, (2, 1)).astype(np.float32),
                np.tile(ss, (2, 1)).astype(np.float32))
    csq, ssq = tables(start + np.arange(Tq) - HQ, False)
    csk, ssk = tables(start + np.arange(Tall) - H, True)
    WLO, WHI = H - HQ - 512, H - HQ
    mask = attn_mask_host(Tq, Tall, WLO, WHI,
                          valid_kv_lo=max(0, H - start), force_lo=H, additive=True)
    e8row = np.zeros((1, 64), np.float32)
    for h in range(8):
        e8row[0, h * 8 + h] = 1.0
    return dict(wqksw=np.ascontiguousarray(wqksw), qs_col=qs_col, ks_col=ks_col,
                csq=csq, ssq=ssq, csk=csk, ssk=ssk, mask=mask, e8row=e8row,
                id8=np.eye(8, dtype=np.float32),
                qs_row=np.ascontiguousarray(qs_l.reshape(1, 64).astype(np.float32)),
                ks_row=np.ascontiguousarray(ks_l.reshape(1, 64).astype(np.float32)))


def col_layout(v, nc_):
    out = np.zeros((128 * nc_,), np.float32)
    out[:len(v)] = v
    return np.ascontiguousarray(out.reshape(nc_, 128).T)


def l0_ffn_b_host(lg, lb, b1, b2, HID=2730):
    MHC = ceil_div(HID, 128)
    return dict(lg_col=col_layout(lg, 8), lb_col=col_layout(lb, 8),
                b1a_col=col_layout(b1[:HID], MHC), b1g_col=col_layout(b1[HID:], MHC),
                b2_col=col_layout(b2, 8))



def build_l0_attn_b_A(T=2048, D=1024, DI=2048, HC=3, stage=9):
    Ta = T + HC
    DIC = DI // 128
    p = Prog()
    pool = PsumPool(p, 6)
    c = p.consts()
    x_in = p.din("x_in", [D, Ta])
    w_in_t = p.din("w_in_t", [D, DI])
    conv_w4 = p.din("conv_w4", [128, DIC * 4])
    conv_b_col = p.din("conv_b_col", [128, DIC])
    wx_dt = p.din("wx_dt", [DI, 64])
    wx_B = p.din("wx_B", [DI, 128])
    wx_C = p.din("wx_C", [DI, 128])
    wdt_t = p.din("wdt_t", [64, 128])
    bdt_col = p.din("bdt_col", [128, 1])
    alog_col = p.din("alog_col", [128, 1])
    o_xis = p.dout("xisilu", [DI, T], BF16)
    o_y0 = p.dout("y0", [1, T])
    o_cp = p.dout("cp", [128, T], BF16)
    o_pend = p.dout("pend", [128, 1])
    o_send = p.dout("send", [128, 1])

    x_get = make_x_loader(p, x_in, Ta, D, nbuf=10, eng="sync")
    xn = p.sbuf("xn", [128, 8 * Ta], BF16)
    xn_toks = emit_ln(p, pool, x_get, xn, Ta, D)
    cw = p.sbuf("cw_sb", [128, DIC * 4], F32)
    cb = p.sbuf("cb_sb", [128, DIC], F32)
    bdt = p.sbuf("bdt_sb", [128, 1], F32)
    alog = p.sbuf("alog_sb", [128, 1], F32)
    cst = [p.dma("gpsimd", cw[:], conv_w4.ap()), p.dma("gpsimd", cb[:], conv_b_col.ap()),
           p.dma("gpsimd", bdt[:], bdt_col.ap()), p.dma("gpsimd", alog[:], alog_col.ap())]

    xi = p.sbuf("xi", [128, DIC * Ta], BF16)
    sil = [p.sbuf(f"sil{i}", [128, 512], BF16) for i in range(3)]
    sil_war = {}
    ctr = {"i": 0}

    def xi_cb(mc, mrows, s0, n, ps, tok_mm):
        t1 = p.op("vector", lambda v, o=xi[0:128, mc * Ta + s0:mc * Ta + s0 + n],
                  i=ps[0:128, 0:n]: v.tensor_copy(o, i), wait=[tok_mm])
        lo = max(s0, HC)
        if lo < s0 + n:
            si = ctr["i"] % 3
            ctr["i"] += 1
            war = sil_war.pop(si, [])
            t2 = p.op("scalar", lambda s, o=sil[si][0:128, 0:n],
                      i=ps[0:128, 0:n]: s.activation(out=o, in_=i, func=AF.Sigmoid),
                      wait=[tok_mm, t1] + war)
            t2b = p.op("vector", lambda v, o=sil[si][0:128, 0:n],
                       i=sil[si][0:128, 0:n],
                       j=xi[0:128, mc * Ta + s0:mc * Ta + s0 + n]:
                       v.tensor_mul(o, i, j), wait=[t2, t1])
            dst = apv(o_xis, mc * 128 * T + (lo - HC), [[T, 128], [1, s0 + n - lo]])
            td = p.dma("sync", dst, sil[si][0:128, lo - s0:n], wait=[t2b], store=True)
            sil_war[si] = [td]
            return [t1, t2, t2b]
        return [t1]

    xi_toks = []
    winb = [p.sbuf(f"winb{i}", [128, 8 * 128], BF16) for i in range(2)]
    win_war = {0: [], 1: []}
    for mc in range(DIC):
        pp = mc % 2
        wtoks = []
        for kc in range(8):
            srcw = apv(w_in_t, kc * 128 * DI + mc * 128, [[DI, 128], [1, 128]])
            wtoks.append(p.dma("gpsimd", winb[pp][0:128, kc * 128:(kc + 1) * 128], srcw,
                               wait=win_war[pp] if kc == 0 else ()))
        mm_all = []
        for (s0, n) in ntiles(Ta, 512):
            bi, ps, w8 = pool.acquire()
            fns = []
            for kc in range(8):
                l = winb[pp][0:128, kc * 128:(kc + 1) * 128]
                rr = xn[0:128, kc * Ta + s0:kc * Ta + s0 + n]
                fns.append(lambda t, a=ps[0:128, 0:n], l=l, rr=rr,
                           st=(kc == 0), sp=(kc == 7): t.matmul(a, l, rr, start=st, stop=sp))
            tok_mm = p.ops("tensor", fns, wait=list(xn_toks) + wtoks + w8)
            mm_all.append(tok_mm)
            rds = xi_cb(mc, 128, s0, n, ps, tok_mm)
            pool.release(bi, list(rds))
            xi_toks += rds
        win_war[pp] = mm_all

    if stage < 1:
        zz = p.sbuf("zz", [128, T], F32)
        tzz = p.op("vector", lambda v: v.memset(zz[:], 0.0), wait=xi_toks)
        p.dma("sync", o_y0.ap(), zz[0:1, 0:T], wait=[tzz], store=True)
        p.dma("sync", o_cp.ap(), xi[0:128, 0:T], wait=[tzz], store=True)
        p.dma("sync", o_pend.ap(), zz[0:128, 0:1], wait=[tzz], store=True)
        p.dma("sync", o_send.ap(), zz[0:128, 0:1], wait=[tzz], store=True)
        p.finish()
        return p.build()
    sc1 = p.sbuf("sc1", [128, T], F32)
    sc2 = p.sbuf("sc2", [128, T], F32)
    acc = [sc1, sc2]
    xc_toks = []
    acc_war = {0: [], 1: []}
    for ci in range(DIC):
        ai = ci % 2
        a = acc[ai]
        fns = []
        t0 = p.op("vector", lambda v, o=a[0:128, 0:T],
                  i=xi[0:128, ci * Ta + 0:ci * Ta + T], s1=cw[0:128, ci * 4:ci * 4 + 1]:
                  v.tensor_single_scalar(o, i, s1, op=ALU.mult),
                  wait=xi_toks + cst + acc_war[ai])
        for j in (1, 2, 3):
            t0 = p.op("vector", lambda v, o=a[0:128, 0:T],
                      i=xi[0:128, ci * Ta + j:ci * Ta + j + T],
                      s1=cw[0:128, ci * 4 + j:ci * 4 + j + 1], j2=a[0:128, 0:T]:
                      v.scalar_tensor_tensor(o, i, s1, j2, op0=ALU.mult, op1=ALU.add),
                      wait=[t0])
        t1 = p.op("vector", lambda v, o=xi[0:128, ci * Ta:ci * Ta + T], i=a[0:128, 0:T],
                  s1=cb[0:128, ci:ci + 1]: v.tensor_scalar_add(o, i, s1), wait=[t0])
        acc_war[ai] = [t1]
        xc_toks.append(t1)

    if stage < 2:
        zz = p.sbuf("zz", [128, T], F32)
        tzz = p.op("vector", lambda v: v.memset(zz[:], 0.0), wait=xc_toks)
        p.dma("sync", o_y0.ap(), zz[0:1, 0:T], wait=[tzz], store=True)
        p.dma("sync", o_cp.ap(), xi[0:128, 0:T], wait=[tzz], store=True)
        p.dma("sync", o_pend.ap(), zz[0:128, 0:1], wait=[tzz], store=True)
        p.dma("sync", o_send.ap(), zz[0:128, 0:1], wait=[tzz], store=True)
        p.finish()
        return p.build()
    wdtp_sb, wdtp_toks = load_weight(p, "wdtp_sb", wx_dt, DI, 64)
    wB_sb, wB_toks = load_weight(p, "wB_sb", wx_B, DI, 128)
    wC_sb, wC_toks = load_weight(p, "wC_sb", wx_C, DI, 128)
    dt_sb = p.sbuf("dt_sb", [64, T], BF16)
    Bm_sb = p.sbuf("Bm_sb", [128, T], F32)
    Cm_sb = p.sbuf("Cm_sb", [128, T], BF16)
    proj_toks = {"dt": [], "B": [], "C": []}
    for (tag, wsb, wtk, M, dst, dt_out) in (("dt", wdtp_sb, wdtp_toks, 64, dt_sb, BF16),
                                            ("B", wB_sb, wB_toks, 128, Bm_sb, F32),
                                            ("C", wC_sb, wC_toks, 128, Cm_sb, BF16)):
        for (s0, n) in ntiles(T, 512):
            bi, ps, w8 = pool.acquire()
            fns = []
            for kc in range(DIC):
                l = wsb[0:128, kc * M:kc * M + M]
                rr = xi[0:128, kc * Ta + s0:kc * Ta + s0 + n]
                fns.append(lambda t, a=ps[0:M, 0:n], l=l, rr=rr,
                           st=(kc == 0), sp=(kc == DIC - 1): t.matmul(a, l, rr, start=st, stop=sp))
            tok_mm = p.ops("tensor", fns, wait=xc_toks + wtk + w8)
            te = p.op("vector", lambda v, o=dst[0:M, s0:s0 + n], i=ps[0:M, 0:n]:
                      v.tensor_copy(o, i), wait=[tok_mm])
            pool.release(bi, te)
            proj_toks[tag].append(te)

    if stage < 3:
        zz = p.sbuf("zz", [128, T], F32)
        tzz = p.op("vector", lambda v: v.memset(zz[:], 0.0),
                   wait=proj_toks["dt"] + proj_toks["B"] + proj_toks["C"])
        p.dma("sync", o_y0.ap(), zz[0:1, 0:T], wait=[tzz], store=True)
        p.dma("sync", o_cp.ap(), Cm_sb[0:128, 0:T], wait=[tzz], store=True)
        p.dma("sync", o_pend.ap(), zz[0:128, 0:1], wait=[tzz], store=True)
        p.dma("sync", o_send.ap(), zz[0:128, 0:1], wait=[tzz], store=True)
        p.finish()
        return p.build()
    wdt_sb, wdt_toks = load_weight(p, "wdt_sb", wdt_t, 64, 128)
    delta = sc1
    srow = p.sbuf("srow", [128, 512], BF16)
    d_toks = []
    srow_war = []
    for (s0, n) in ntiles(T, 512):
        bi, ps, w8 = pool.acquire()
        tok_mm = p.op("tensor", lambda t, a=ps[0:128, 0:n], l=wdt_sb[0:64, 0:128],
                      rr=dt_sb[0:64, s0:s0 + n]: t.matmul(a, l, rr, start=True, stop=True),
                      wait=proj_toks["dt"] + wdt_toks + w8)
        t1 = p.op("scalar", lambda s, o=srow[0:128, 0:n], i=ps[0:128, 0:n],
                  b=bdt[0:128, 0:1]: s.activation(out=o, in_=i, func=AF.Abs, bias=b),
                  wait=[tok_mm] + cst + srow_war)
        t2 = p.op("scalar", lambda s, o=srow[0:128, 0:n], i=srow[0:128, 0:n]:
                  s.activation(out=o, in_=i, func=AF.Exp, scale=-1.0), wait=[t1])
        t3 = p.op("scalar", lambda s, o=srow[0:128, 0:n], i=srow[0:128, 0:n]:
                  s.activation(out=o, in_=i, func=AF.Ln, bias=1.0), wait=[t2])
        t4 = p.op("vector", lambda v, o=delta[0:128, s0:s0 + n], i=ps[0:128, 0:n],
                  s1=bdt[0:128, 0:1]: v.tensor_scalar(o, i, s1, 0.0, op0=ALU.add,
                                                      op1=ALU.max), wait=[tok_mm, t1])
        t5 = p.op("vector", lambda v, o=delta[0:128, s0:s0 + n],
                  i=delta[0:128, s0:s0 + n], j=srow[0:128, 0:n]: v.tensor_add(o, i, j),
                  wait=[t3, t4])
        pool.release(bi, t5)
        d_toks.append(t5)
        srow_war = [t5]

    if stage < 4:
        zz = p.sbuf("zz", [128, T], F32)
        tzz = p.op("vector", lambda v: v.memset(zz[:], 0.0), wait=d_toks)
        p.dma("sync", o_y0.ap(), delta[0:1, 0:T], wait=[tzz], store=True)
        p.dma("sync", o_cp.ap(), Cm_sb[0:128, 0:T], wait=[tzz], store=True)
        p.dma("sync", o_pend.ap(), zz[0:128, 0:1], wait=[tzz], store=True)
        p.dma("sync", o_send.ap(), zz[0:128, 0:1], wait=[tzz], store=True)
        p.finish()
        return p.build()
    Acol = p.sbuf("Acol", [128, 1], F32)
    tA1 = p.op("scalar", lambda s: s.activation(out=Acol[:], in_=alog[:], func=AF.Exp),
               wait=cst)
    tA2 = p.op("vector", lambda v: v.tensor_scalar_mul(Acol[:], Acol[:], -1.0), wait=[tA1])
    Abar = sc2
    Bbar = Bm_sb
    ab_toks = []
    for (s0, n) in ntiles(T, 512):
        t2 = p.op("vector", lambda v, o=Bbar[0:128, s0:s0 + n], i=delta[0:128, s0:s0 + n],
                  j=Bm_sb[0:128, s0:s0 + n]: v.tensor_mul(o, i, j),
                  wait=d_toks + proj_toks["B"])
        t1 = p.op("scalar", lambda s, o=Abar[0:128, s0:s0 + n], i=delta[0:128, s0:s0 + n]:
                  s.activation(out=o, in_=i, func=AF.Exp, scale=Acol[0:128, 0:1]),
                  wait=d_toks + [tA2])
        ab_toks += [t1, t2]

    if stage < 5:
        zz = p.sbuf("zz", [128, T], F32)
        tzz = p.op("vector", lambda v: v.memset(zz[:], 0.0), wait=ab_toks)
        p.dma("sync", o_y0.ap(), Abar[0:1, 0:T], wait=[tzz], store=True)
        p.dma("sync", o_cp.ap(), Cm_sb[0:128, 0:T], wait=[tzz], store=True)
        p.dma("sync", o_pend.ap(), zz[0:128, 0:1], wait=[tzz], store=True)
        p.dma("sync", o_send.ap(), zz[0:128, 0:1], wait=[tzz], store=True)
        p.finish()
        return p.build()
    h0 = delta
    Pp = p.sbuf("Pp", [128, T], F32)
    zcol = p.sbuf("zcol", [128, 1], F32)
    tz = p.op("vector", lambda v: v.memset(zcol[:], 0.0))
    zbc = apv(zcol, 0, [[1, 128], [0, T]])
    ts1 = p.op("vector", lambda v: v.tensor_tensor_scan(
        out=h0[0:128, 0:T], data0=Abar[0:128, 0:T], data1=Bbar[0:128, 0:T],
        initial=0.0, op0=ALU.mult, op1=ALU.add), wait=ab_toks)
    ts2 = p.op("vector", lambda v: v.tensor_tensor_scan(
        out=Pp[0:128, 0:T], data0=Abar[0:128, 0:T], data1=zbc,
        initial=1.0, op0=ALU.mult, op1=ALU.add), wait=ab_toks + [tz])
    tpe = p.dma("sync", o_pend.ap(), Pp[0:128, T - 1:T], wait=[ts2], store=True)
    tse = p.dma("sync", o_send.ap(), h0[0:128, T - 1:T], wait=[ts1], store=True)
    cpb = p.sbuf("cpb", [128, T], BF16)
    tcp = p.op("vector", lambda v: v.tensor_mul(cpb[0:128, 0:T], Pp[0:128, 0:T],
                                                Cm_sb[0:128, 0:T]),
               wait=[ts2] + proj_toks["C"])
    tcpd = p.dma("sync", o_cp.ap(), cpb[0:128, 0:T], wait=[tcp], store=True)
    ub = p.sbuf("ub", [128, T], BF16)
    tu = p.op("vector", lambda v: v.tensor_mul(ub[0:128, 0:T], h0[0:128, 0:T],
                                               Cm_sb[0:128, 0:T]),
              wait=[ts1] + proj_toks["C"])
    y0r = p.sbuf("y0r", [1, T], F32)
    y0_toks = []
    for (s0, n) in ntiles(T, 512):
        bi, ps, w8 = pool.acquire()
        tm = p.op("tensor", lambda t, a=ps[0:1, 0:n], l=c["ones_col_bf"][0:128, 0:1],
                  rr=ub[0:128, s0:s0 + n]: t.matmul(a, l, rr, start=True, stop=True),
                  wait=[tu] + w8 + c["tok"])
        te = p.op("vector", lambda v, o=y0r[0:1, s0:s0 + n], i=ps[0:1, 0:n]:
                  v.tensor_copy(o, i), wait=[tm])
        pool.release(bi, te)
        y0_toks.append(te)
    p.dma("sync", o_y0.ap(), y0r[0:1, 0:T], wait=y0_toks, store=True)
    p.finish()
    return p.build()


def build_l0_attn_b_B(T=2048, D=1024, DI=2048):
    DIC = DI // 128
    p = Prog()
    pool = PsumPool(p, 6)
    c = p.consts()
    x_in = p.din("x_in", [D, T])
    xis_in = p.din("xisilu", [DI, T], BF16)
    y0_in = p.din("y0", [1, T])
    cp_in = p.din("cp", [128, T], BF16)
    ps_in = p.din("ps_all", [128, 8])
    ss_in = p.din("ss_all", [128, 8])
    gfl_in = p.din("gfl", [128, 8])
    omg_in = p.din("omg", [128, 8])
    wg_in = p.din("w_gate", [1, 1])
    wout = p.din("w_out_t", [DI, D])
    y = p.dout("y", [D, T])

    psb = p.sbuf("psb", [128, 8], F32)
    ssb = p.sbuf("ssb", [128, 8], F32)
    gfl = p.sbuf("gfl_sb", [128, 8], F32)
    omg = p.sbuf("omg_sb", [128, 8], F32)
    wg = p.sbuf("wg_sb", [1, 1], F32)
    y0r = p.sbuf("y0r", [1, T], F32)
    cpb = p.sbuf("cpb", [128, T], BF16)
    cst = [p.dma("gpsimd", psb[:], ps_in.ap()), p.dma("gpsimd", ssb[:], ss_in.ap()),
           p.dma("gpsimd", gfl[:], gfl_in.ap()), p.dma("gpsimd", omg[:], omg_in.ap()),
           p.dma("gpsimd", wg[:], wg_in.ap()), p.dma("gpsimd", y0r[:], y0_in.ap()),
           p.dma("gpsimd", cpb[:], cp_in.ap())]
    t1 = p.op("vector", lambda v: v.tensor_mul(psb[:], psb[:], gfl[:]), wait=cst)
    t2 = p.op("vector", lambda v: v.tensor_add(psb[:], psb[:], omg[:]), wait=[t1])
    t3 = p.op("vector", lambda v: v.tensor_mul(ssb[:], ssb[:], gfl[:]), wait=cst)
    hin = p.sbuf("hin", [128, 2], F32)
    th = p.op("vector", lambda v: v.memset(hin[0:128, 0:1], 0.0))
    for j in range(8):
        th = p.op("vector", lambda v, j=j: v.scalar_tensor_tensor(
            hin[0:128, 0:1], hin[0:128, 0:1], psb[0:128, j:j + 1],
            ssb[0:128, j:j + 1], op0=ALU.mult, op1=ALU.add), wait=[th, t2, t3])
    thb = p.op("vector", lambda v: v.tensor_copy(hin[0:128, 1:2].bitcast(F32),
                                                 hin[0:128, 0:1]), wait=[th])
    hin_bf = p.sbuf("hin_bf", [128, 1], BF16)
    thc = p.op("vector", lambda v: v.tensor_copy(hin_bf[:], hin[0:128, 0:1]), wait=[th])
    gate_bc = p.sbuf("gate_bc", [128, T], BF16)
    grow = p.sbuf("grow", [1, T], F32)
    gb_toks = []
    for (s0, n) in ntiles(T, 512):
        bi, ps, w8 = pool.acquire()
        tm = p.op("tensor", lambda t, a=ps[0:1, 0:n], l=hin_bf[0:128, 0:1],
                  rr=cpb[0:128, s0:s0 + n]: t.matmul(a, l, rr, start=True, stop=True),
                  wait=[thc] + cst + w8)
        te = p.op("vector", lambda v, o=grow[0:1, s0:s0 + n], i=ps[0:1, 0:n],
                  j=y0r[0:1, s0:s0 + n]: v.tensor_add(o, i, j), wait=[tm])
        te2 = p.op("vector", lambda v, o=grow[0:1, s0:s0 + n], i=grow[0:1, s0:s0 + n],
                   s1=wg[0:1, 0:1]: v.tensor_single_scalar(o, i, s1, op=ALU.mult),
                   wait=[te])
        pool.release(bi, te2)
        te3 = p.op("vector", lambda v, o=grow[0:1, s0:s0 + n].bitcast(F32) if False
                   else grow[0:1, s0:s0 + n], i=grow[0:1, s0:s0 + n]: v.tensor_copy(o, i),
                   wait=[te2], sig=False) if False else None
        bi2, ps2, w82 = pool.acquire()
        tbf = p.op("vector", lambda v, o=gate_bc[0:1, s0:s0 + n] if False else None: None,
                   sig=False) if False else None
        growb = p.sbuf(f"growb{s0}", [1, 512], BF16)
        tb = p.op("vector", lambda v, o=growb[0:1, 0:n], i=grow[0:1, s0:s0 + n]:
                  v.tensor_copy(o, i), wait=[te2])
        tmm = p.op("tensor", lambda t, a=ps2[0:128, 0:n], l=c["ones_row_bf"][0:1, 0:128],
                   rr=growb[0:1, 0:n]: t.matmul(a, l, rr, start=True, stop=True),
                   wait=[tb] + w82)
        tev = p.op("vector", lambda v, o=gate_bc[0:128, s0:s0 + n], i=ps2[0:128, 0:n]:
                   v.tensor_copy(o, i), wait=[tmm])
        pool.release(bi2, tev)
        gb_toks.append(tev)
    xis = p.sbuf("xis", [128, DIC * T], BF16)
    xis_toks = []
    for ci in range(DIC):
        src = apv(xis_in, ci * 128 * T, [[T, 128], [1, T]])
        xis_toks.append(p.dma("gpsimd", xis[0:128, ci * T:(ci + 1) * T], src))
    op_toks = []
    for ci in range(DIC):
        t1_ = p.op("vector", lambda v, o=xis[0:128, ci * T:(ci + 1) * T],
                   i=xis[0:128, ci * T:(ci + 1) * T], j=gate_bc[0:128, 0:T]:
                   v.tensor_mul(o, i, j), wait=gb_toks + xis_toks)
        op_toks.append(t1_)
    wo_sb, wo_toks = load_weight(p, "wo_sb", wout, DI, D)
    x_get = make_x_loader(p, x_in, T, D, nbuf=6, eng="sync")
    y_sb = [p.sbuf(f"ysb{i}", [128, 512], F32) for i in range(4)]
    y_war = {}
    ctr = {"i": 0}

    def out_cb(mc, mrows, s0, n, ps, tok_mm):
        xi_ = ctr["i"] % 4
        ctr["i"] += 1
        xap, xw = x_get(mc, s0, n)
        war = y_war.pop(xi_, [])
        t1_ = p.op("vector", lambda v, o=y_sb[xi_][0:mrows, 0:n], i=ps[0:mrows, 0:n],
                   j=xap: v.tensor_add(o, i, j), wait=[tok_mm] + xw + war)
        x_get.note_reader(mc, s0, n, t1_)
        dst = apv(y, mc * 128 * T + s0, [[T, mrows], [1, n]])
        td = p.dma("sync", dst, y_sb[xi_][0:mrows, 0:n], wait=[t1_], store=True)
        y_war[xi_] = [td]
        return t1_

    emit_proj(p, pool, wo_sb, wo_toks, xis, 0, T, op_toks, DI, D, T, out_cb)
    p.finish()
    return p.build()



def build_l1_attn_a_A(T=1024, D=1024):
    Ta = T + 2
    p = Prog()
    pool = PsumPool(p, 6)
    c = p.consts()
    x_in = p.din("x_in", [D, Ta])
    cw_in = p.din("conv3_col", [128, 8 * 3])
    pe_w_t = p.din("pe_w_t", [D, D])
    qkv_w_t = p.din("qkv_w_t", [D, 3 * D])
    o_q = p.dout("q_f", [D, T], BF16)
    o_k = p.dout("k_f", [D, T], BF16)
    o_v = p.dout("v_t", [T, D], BF16)

    x_get = make_x_loader(p, x_in, Ta, D, nbuf=8, eng="sync")
    xn = p.sbuf("xn", [128, 8 * Ta], BF16)
    xn_toks = emit_ln(p, pool, x_get, xn, Ta, D)
    cw = p.sbuf("cw_sb", [128, 8 * 3], F32)
    cst = [p.dma("gpsimd", cw[:], cw_in.ap())]
    xs = p.sbuf("xs", [128, 8 * T], BF16)
    acc = [p.sbuf(f"acc{i}", [128, T], F32) for i in range(2)]
    acc_war = {0: [], 1: []}
    xs_toks = []
    for ci in range(8):
        ai = ci % 2
        a = acc[ai]
        t0 = p.op("vector", lambda v, o=a[0:128, 0:T], i=xn[0:128, ci * Ta:ci * Ta + T],
                  s1=cw[0:128, ci * 3:ci * 3 + 1]:
                  v.tensor_single_scalar(o, i, s1, op=ALU.mult),
                  wait=list(xn_toks) + cst + acc_war[ai])
        for j in (1, 2):
            t0 = p.op("vector", lambda v, o=a[0:128, 0:T],
                      i=xn[0:128, ci * Ta + j:ci * Ta + j + T],
                      s1=cw[0:128, ci * 3 + j:ci * 3 + j + 1], j2=a[0:128, 0:T]:
                      v.scalar_tensor_tensor(o, i, s1, j2, op0=ALU.mult, op1=ALU.add),
                      wait=[t0])
        t1 = p.op("vector", lambda v, o=xs[0:128, ci * T:(ci + 1) * T], i=a[0:128, 0:T]:
                  v.tensor_copy(o, i), wait=[t0])
        acc_war[ai] = [t1]
        xs_toks.append(t1)
    xs2 = p.sbuf("xs2", [128, 8 * T], BF16)
    xs2_toks = []

    def pe_cb(mc, mrows, s0, n, ps, tok_mm):
        t1 = p.op("vector", lambda v, o=xs2[0:mrows, mc * T + s0:mc * T + s0 + n],
                  i=ps[0:mrows, 0:n], j=xs[0:mrows, mc * T + s0:mc * T + s0 + n]:
                  v.tensor_add(o, i, j), wait=[tok_mm] + xs_toks)
        xs2_toks.append(t1)
        return t1
    emit_proj_stream(p, pool, pe_w_t, D, xs, 0, T, xs_toks, D, D, T, pe_cb, tag="pe")
    ob = [p.sbuf(f"ob{i}", [128, 512], BF16) for i in range(3)]
    ob_war = {}
    ctr = {"i": 0}
    for (part, dst) in ((0, o_q), (1, o_k)):
        def qk_cb(mc, mrows, s0, n, ps, tok_mm, dst=dst):
            oi = ctr["i"] % 3
            ctr["i"] += 1
            war = ob_war.pop(oi, [])
            t1 = p.op("vector", lambda v, o=ob[oi][0:mrows, 0:n], i=ps[0:mrows, 0:n]:
                      v.tensor_copy(o, i), wait=[tok_mm] + war)
            d = apv(dst, mc * 128 * T + s0, [[T, mrows], [1, n]])
            td = p.dma("sync", d, ob[oi][0:mrows, 0:n], wait=[t1], store=True)
            ob_war[oi] = [td]
            return t1
        u = p.uid()
        wb = [p.sbuf(f"qkw{u}_{i}", [128, 8 * 128], BF16) for i in range(2)]
        war_w = {0: [], 1: []}
        for mc in range(8):
            pp = mc % 2
            wtoks = []
            for kc in range(8):
                srcw = apv(qkv_w_t, kc * 128 * 3 * D + part * D + mc * 128,
                           [[3 * D, 128], [1, 128]])
                wtoks.append(p.dma("gpsimd", wb[pp][0:128, kc * 128:(kc + 1) * 128],
                                   srcw, wait=war_w[pp] if kc == 0 else ()))
            mms = []
            for (s0, n) in ntiles(T, 512):
                bi, ps, w8 = pool.acquire()
                fns = []
                for kc in range(8):
                    l = wb[pp][0:128, kc * 128:(kc + 1) * 128]
                    rr = xs2[0:128, kc * T + s0:kc * T + s0 + n]
                    fns.append(lambda t, a=ps[0:128, 0:n], l=l, rr=rr,
                               st=(kc == 0), sp=(kc == 7): t.matmul(a, l, rr, start=st, stop=sp))
                tok_mm = p.ops("tensor", fns, wait=xs2_toks + wtoks + w8)
                rd = qk_cb(mc, 128, s0, n, ps, tok_mm)
                pool.release(bi, rd)
                mms.append(tok_mm)
            war_w[pp] = mms
    wv2 = p.sbuf("wv2", [128, 8 * 512], BF16)
    for dv2 in range(2):
        wtoks = []
        for kc in range(8):
            srcw = apv(qkv_w_t, kc * 128 * 3 * D + 2 * D + dv2 * 512, [[3 * D, 128], [1, 512]])
            wtoks.append(p.dma("gpsimd", wv2[0:128, kc * 512:(kc + 1) * 512], srcw,
                               wait=p._v_war if kc == 0 and hasattr(p, "_v_war") else ()))
        mms = []
        for tt in range(T // 128):
            bi, ps, w8 = pool.acquire()
            fns = []
            for kc in range(8):
                l = xs2[0:128, kc * T + tt * 128:kc * T + tt * 128 + 128]
                rr = wv2[0:128, kc * 512:(kc + 1) * 512]
                fns.append(lambda t, a=ps[0:128, 0:512], l=l, rr=rr,
                           st=(kc == 0), sp=(kc == 7): t.matmul(a, l, rr, start=st, stop=sp))
            tok_mm = p.ops("tensor", fns, wait=xs2_toks + wtoks + w8)
            oi = ctr["i"] % 3
            ctr["i"] += 1
            war = ob_war.pop(oi, [])
            t1 = p.op("vector", lambda v, o=ob[oi][0:128, 0:512], i=ps[0:128, 0:512]:
                      v.tensor_copy(o, i), wait=[tok_mm] + war)
            d = apv(o_v, tt * 128 * D + dv2 * 512, [[D, 128], [1, 512]])
            td = p.dma("sync", d, ob[oi][0:128, 0:512], wait=[t1], store=True)
            ob_war[oi] = [td]
            pool.release(bi, t1)
            mms.append(tok_mm)
        p._v_war = mms
    p.finish()
    return p.build()


def build_l1_attn_a_B(T=2048, D=1024, S=8192, NH=16):
    p = Prog()
    pool = PsumPool(p, 6)
    c = p.consts()
    q_in = p.din("q_f", [D, T], BF16)
    k_in = p.din("k_f", [D, S], BF16)
    v_in = p.din("v_t", [S, D], BF16)
    x_in = p.din("x_in", [D, T])
    out_w_t = p.din("out_w_t", [D, D])
    y = p.dout("y", [D, T])

    NB = S // 1024
    SH = S // 2
    ksc = p.sbuf("ksc", [128, SH], BF16)
    kmx = p.sbuf("kmx", [1, 16], F32)
    nrow = p.sbuf("nrow", [1, S], BF16)
    kn_toks = []
    km_toks = []
    for cchunk in range(8):
        mm_toks = {0: [], 1: []}
        for sh in range(2):
            srck = apv(k_in, cchunk * 128 * S + sh * SH, [[S, 128], [1, SH]])
            tl = p.dma("gpsimd", ksc[0:128, 0:SH], srck,
                       wait=kn_toks[-3:] if kn_toks else ())
            tsq = p.op("scalar", lambda s: s.activation(out=ksc[0:128, 0:SH],
                                                        in_=ksc[0:128, 0:SH],
                                                        func=AF.Square), wait=[tl])
            for half in range(2):
                bp = half * 64
                for (s0, n) in ntiles(SH, 512):
                    bi, ps, w8 = pool.acquire()
                    t_mm = p.op("tensor", lambda t, a=ps[0:1, 0:n],
                                l=c["ones_col_bf"][bp:bp + 64, 0:1],
                                rr=ksc[bp:bp + 64, s0:s0 + n]:
                                t.matmul(a, l, rr, start=True, stop=True),
                                wait=[tsq] + w8 + c["tok"])
                    t_cp = p.op("vector", lambda v, o=nrow[0:1, sh * SH + s0:sh * SH + s0 + n],
                                i=ps[0:1, 0:n]: v.tensor_copy(o, i), wait=[t_mm])
                    pool.release(bi, t_cp)
                    mm_toks[half].append(t_cp)
                    kn_toks.append(t_mm)
        for half in range(2):
            h = cchunk * 2 + half
            t_mx = p.op("vector", lambda v, o=kmx[0:1, h:h + 1], i=nrow[0:1, 0:S]:
                        v.tensor_reduce(o, i, axis=AX.X, op=ALU.max), wait=mm_toks[half])
            kn_toks.append(t_mx)
            km_toks.append(t_mx)
    t_kx = p.op("scalar", lambda s: s.activation(out=kmx[0:1, 0:16], in_=kmx[0:1, 0:16],
                                                 func=AF.Sqrt), wait=km_toks)

    qaug = p.sbuf("qaug", [65, 8 * T], BF16)
    kaug = p.sbuf("kaug", [65, 8 * 1024], BF16)
    vblk = p.sbuf("vblk", [128, 8 * 1024], BF16)
    o_acc = p.sbuf("o_acc", [128, 8 * T], F32)
    o_int = p.dint("o_scratch", [D, T], BF16)
    ob3 = [p.sbuf(f"ob3{i}", [128, 512], BF16) for i in range(3)]
    ob3_war = {}
    obc = {"i": 0}
    pt = [p.sbuf(f"pt{i}", [128, 512], BF16) for i in range(2)]
    nq = p.sbuf("nq", [1, T], F32)
    brow = p.sbuf("brow", [1, T], F32)
    rec_row = p.sbuf("rec_row", [1, 512], F32)
    rec_bf = p.sbuf("rec_bf", [1, 512], BF16)
    bc_o = p.sbuf("bc_o", [64, 512], BF16)
    t_vones = p.op("vector", lambda v: v.memset(
        sbv(vblk, 8 * 1024, 128, 64, [[128, 64], [1, 64]]), 1.0))
    t_kones = p.op("vector", lambda v: v.memset(kaug[64:65, 0:8 * 1024], 1.0))

    epi = {"prev_t3": None}
    prev_grp = []
    ofm_toks = []
    for g in range(2):
        q_toks = []
        for hh in range(8):
            h = g * 8 + hh
            src = apv(q_in, (g * 512 + hh * 64) * T, [[T, 64], [1, T]])
            tq = p.dma("gpsimd", qaug[0:64, hh * T:(hh + 1) * T], src, wait=prev_grp)
            tsq = p.op("scalar", lambda s, o=ksc[0:64, 0:T],
                       i=qaug[0:64, hh * T:(hh + 1) * T]:
                       s.activation(out=o, in_=i, func=AF.Square),
                       wait=[tq] + kn_toks + prev_grp)
            mmq = []
            for (s0, n) in ntiles(T, 512):
                bi, ps, w8 = pool.acquire()
                t_mm = p.op("tensor", lambda t, a=ps[0:1, 0:n],
                            l=c["ones_col_bf"][0:64, 0:1], rr=ksc[0:64, s0:s0 + n]:
                            t.matmul(a, l, rr, start=True, stop=True), wait=[tsq] + w8)
                t_cp = p.op("vector", lambda v, o=nq[0:1, s0:s0 + n], i=ps[0:1, 0:n]:
                            v.tensor_copy(o, i), wait=[t_mm])
                pool.release(bi, t_cp)
                mmq.append(t_cp)
                kn_toks.append(t_mm)
            tb1 = p.op("scalar", lambda s, o=brow[0:1, 0:T], i=nq[0:1, 0:T]:
                       s.activation(out=o, in_=i, func=AF.Sqrt), wait=mmq)
            tb2 = p.op("vector", lambda v, o=brow[0:1, 0:T], i=brow[0:1, 0:T],
                       s1=kmx[0:1, h:h + 1]: v.tensor_single_scalar(o, i, s1, op=ALU.mult),
                       wait=[tb1, t_kx])
            tb3 = p.op("vector", lambda v, o=qaug[64:65, hh * T:(hh + 1) * T],
                       i=brow[0:1, 0:T]: v.tensor_scalar_mul(o, i, -1.0), wait=[tb2])
            q_toks += [tq, tb3]
        tz = p.op("vector", lambda v: v.memset(o_acc[:], 0.0), wait=prev_grp)
        acc_toks = [tz]
        for kb in range(NB):
            kb_toks = []
            for hh in range(8):
                h = g * 8 + hh
                src = apv(k_in, (g * 512 + hh * 64) * S + kb * 1024, [[S, 64], [1, 1024]])
                kb_toks.append(p.dma("gpsimd", kaug[0:64, hh * 1024:(hh + 1) * 1024],
                                     src, wait=acc_toks[-2:] if len(acc_toks) > 1 else list(prev_grp)))
            vb_toks = []
            for tt in range(8):
                src = apv(v_in, (kb * 1024 + tt * 128) * D + g * 512,
                          [[D, 128], [64, 8], [1, 64]])
                dstv = sbv(vblk, 8 * 1024, 128, tt * 1024, [[128, 8], [1, 64]])
                vb_toks.append(p.dma("sync", dstv, src,
                                     wait=acc_toks[-2:] if len(acc_toks) > 1 else list(prev_grp)))
            new_acc = []
            for hh in range(8):
                for (c0, n) in ntiles(T, 512):
                    bi, psB, w8 = pool.acquire()
                    fns_all = []
                    ptk = []
                    for tt in range(8):
                        bi2, psS, w82 = pool.acquire()
                        t_mm = p.op("tensor", lambda t, a=psS[0:128, 0:n],
                                    l=kaug[0:65, hh * 1024 + tt * 128:hh * 1024 + tt * 128 + 128],
                                    rr=qaug[0:65, hh * T + c0:hh * T + c0 + n]:
                                    t.matmul(a, l, rr, start=True, stop=True),
                                    wait=kb_toks + q_toks + [t_kones] + w82)
                        pi = tt % 2
                        t_e = p.op("scalar", lambda s, o=pt[pi][0:128, 0:n],
                                   i=psS[0:128, 0:n], sc=0.125:
                                   s.activation(out=o, in_=i, func=AF.Exp, scale=sc),
                                   wait=[t_mm] + (ptk[-2:-1] if len(ptk) >= 2 else []))
                        pool.release(bi2, t_e)
                        t_av = p.op("tensor", lambda t, a=psB[0:128, 0:n],
                                    l=vblk[0:128, tt * 1024 + hh * 128:tt * 1024 + hh * 128 + 128],
                                    rr=pt[pi][0:128, 0:n], st=(tt == 0), sp=(tt == 7):
                                    t.matmul(a, l, rr, start=st, stop=sp),
                                    wait=[t_e, t_vones] + vb_toks + w8)
                        ptk.append(t_av)
                    t_add = p.op("vector", lambda v, o=o_acc[0:128, hh * T + c0:hh * T + c0 + n],
                                 i=o_acc[0:128, hh * T + c0:hh * T + c0 + n],
                                 j=psB[0:128, 0:n]: v.tensor_add(o, i, j),
                                 wait=[ptk[-1], tz] + acc_toks[-1:])
                    pool.release(bi, t_add)
                    new_acc.append(t_add)
            acc_toks = acc_toks + new_acc
        grp_toks = []
        for hh in range(8):
            for (c0, n) in ntiles(T, 512):
                war3 = [epi["prev_t3"]] if epi["prev_t3"] else []
                t1 = p.op("vector", lambda v, o=rec_row[0:1, 0:n],
                          i=o_acc[64:65, hh * T + c0:hh * T + c0 + n]:
                          v.reciprocal(o, i), wait=acc_toks)
                t2 = p.op("vector", lambda v, o=rec_bf[0:1, 0:n], i=rec_row[0:1, 0:n]:
                          v.tensor_copy(o, i), wait=[t1] + war3)
                bi2, psC, w82 = pool.acquire()
                t3 = p.op("tensor", lambda t, a=psC[0:64, 0:n],
                          l=c["ones_row_bf"][0:1, 0:64], rr=rec_bf[0:1, 0:n]:
                          t.matmul(a, l, rr, start=True, stop=True), wait=[t2] + w82)
                epi["prev_t3"] = t3
                t4 = p.op("vector", lambda v, o=bc_o[0:64, 0:n], i=psC[0:64, 0:n]:
                          v.tensor_copy(o, i), wait=[t3])
                pool.release(bi2, t4)
                oi = obc["i"] % 3
                obc["i"] += 1
                owar = ob3_war.pop(oi, [])
                t5 = p.op("vector", lambda v, o=ob3[oi][0:64, 0:n],
                          i=o_acc[0:64, hh * T + c0:hh * T + c0 + n],
                          j=bc_o[0:64, 0:n]: v.tensor_mul(o, i, j), wait=[t4] + owar)
                dsto = apv(o_int, (g * 512 + hh * 64) * T + c0, [[T, 64], [1, n]])
                td = p.dma("sync", dsto, ob3[oi][0:64, 0:n], wait=[t5])
                ob3_war[oi] = [td]
                grp_toks.append(t5)
                ofm_toks.append(td)
        prev_grp = grp_toks
    x_get = make_x_loader(p, x_in, T, D, nbuf=3, eng="sync")
    y_sb = [p.sbuf(f"ysb{i}", [128, 512], F32) for i in range(2)]
    y_war = {}
    ctr = {"i": 0}

    def out_cb(mc, mrows, s0, n, ps, tok_mm):
        xi_ = ctr["i"] % 2
        ctr["i"] += 1
        xap, xw = x_get(mc, s0, n)
        war = y_war.pop(xi_, [])
        t1 = p.op("vector", lambda v, o=y_sb[xi_][0:mrows, 0:n], i=ps[0:mrows, 0:n],
                  j=xap: v.tensor_add(o, i, j), wait=[tok_mm] + xw + war)
        x_get.note_reader(mc, s0, n, t1)
        dst = apv(y, mc * 128 * T + s0, [[T, mrows], [1, n]])
        td = p.dma("sync", dst, y_sb[xi_][0:mrows, 0:n], wait=[t1], store=True)
        y_war[xi_] = [td]
        return t1

    o_get = make_x_loader(p, o_int, T, D, nbuf=9, tag="old", eng="gpsimd", dtype=BF16)
    p.wait("gpsimd", ofm_toks)
    emit_proj_stream_ld(p, pool, out_w_t, D, o_get, T, D, D, T, out_cb)
    p.finish()
    return p.build()



def build_l1_ffn_A(T=2048, D=1024, E=8):
    p = Prog()
    pool = PsumPool(p, 6)
    c = p.consts()
    x_in = p.din("x_in", [D, T])
    wg_in = p.din("wg", [D, E])
    o_p = p.dout("probs", [T, E])
    x_get = make_x_loader(p, x_in, T, D, nbuf=10, eng="sync")
    xn = p.sbuf("xn", [128, 8 * T], BF16)
    xn_toks = emit_ln(p, pool, x_get, xn, T, D)
    wg = p.sbuf("wg_sb", [128, 8 * E], BF16)
    wt = [p.dma("gpsimd", wg[0:128, kc * E:(kc + 1) * E],
                apv(wg_in, kc * 128 * E, [[E, 128], [1, E]])) for kc in range(8)]
    pr = [p.sbuf(f"pr{i}", [128, E], F32) for i in range(3)]
    mx = p.sbuf("mxq", [128, 1], F32)
    sm = p.sbuf("smq", [128, 1], F32)
    pw = {}
    for tt in range(T // 128):
        bi, ps, w8 = pool.acquire()
        fns = []
        for kc in range(8):
            l = xn[0:128, kc * T + tt * 128:kc * T + tt * 128 + 128]
            rr = wg[0:128, kc * E:(kc + 1) * E]
            fns.append(lambda t, a=ps[0:128, 0:E], l=l, rr=rr,
                       st=(kc == 0), sp=(kc == 7): t.matmul(a, l, rr, start=st, stop=sp))
        tok_mm = p.ops("tensor", fns, wait=list(xn_toks) + wt + w8)
        pi = tt % 3
        war = pw.pop(pi, [])
        t0 = p.op("vector", lambda v, o=pr[pi][0:128, 0:E], i=ps[0:128, 0:E]:
                  v.tensor_copy(o, i), wait=[tok_mm] + war)
        pool.release(bi, t0)
        t1 = p.ops("vector", [
            lambda v, o=mx[0:128, 0:1], i=pr[pi][0:128, 0:E]:
            v.tensor_reduce(o, i, axis=AX.X, op=ALU.max),
            lambda v, o=mx[0:128, 0:1], i=mx[0:128, 0:1]:
            v.tensor_scalar_mul(o, i, -1.0),
        ], wait=[t0], chain=True)
        t2 = p.op("scalar", lambda s, o=pr[pi][0:128, 0:E], i=pr[pi][0:128, 0:E],
                  b=mx[0:128, 0:1]: s.activation(out=o, in_=i, func=AF.Exp,
                                                 bias=b, accum_out=sm[0:128, 0:1]),
                  wait=[t1])
        t3 = p.op("vector", lambda v, o=sm[0:128, 0:1], i=sm[0:128, 0:1]:
                  v.reciprocal(o, i), wait=[t2])
        t4 = p.op("vector", lambda v, o=pr[pi][0:128, 0:E], i=pr[pi][0:128, 0:E],
                  s1=sm[0:128, 0:1]: v.tensor_single_scalar(o, i, s1, op=ALU.mult),
                  wait=[t3])
        td = p.dma("sync", apv(o_p, tt * 128 * E, [[E, 128], [1, E]]),
                   pr[pi][0:128, 0:E], wait=[t4], store=True)
        pw[pi] = [td]
    p.finish()
    return p.build()


def build_l1_ffn_B(T=1024, D=1024, E=8, F=2048, TT=16384):
    p = Prog()
    pool = PsumPool(p, 6)
    c = p.consts()
    x_in = p.din("x_in", [D, T])
    pall_in = p.din("probs_all", [TT, E])
    pown_in = p.din("probs_own", [T, E])
    w1_in = p.din("w1", [E, D, F])
    w2_in = p.din("w2", [E, F, D])
    e_in = p.din("e_row", [128, E])
    em_in = p.din("em999", [128, E])
    id_in = p.din("id128", [128, 128])
    id8_in = p.din("id8", [8, 8])
    y = p.dout("y", [D, T])

    x_get = make_x_loader(p, x_in, T, D, nbuf=10, eng="sync")
    xn = p.sbuf("xn", [128, 8 * T], BF16)
    xn_toks = emit_ln(p, pool, x_get, xn, T, D)
    NTt = TT // 128
    pb = p.sbuf("pb", [128, NTt * E], F32)
    pb_toks = [p.dma("gpsimd", pb[0:128, tt * E:(tt + 1) * E],
                     apv(pall_in, tt * 128 * E, [[E, 128], [1, E]]))
               for tt in range(NTt)]
    erow = p.sbuf("erow_sb", [128, E], F32)
    em = p.sbuf("em_sb", [128, E], F32)
    idn = p.sbuf("idn", [128, 128], BF16)
    id8b = p.sbuf("id8b", [8, 8], BF16)
    cst = [p.dma("gpsimd", erow[:], e_in.ap()), p.dma("gpsimd", em[:], em_in.ap()),
           p.dma("gpsimd", idn[:], id_in.ap()), p.dma("gpsimd", id8b[:], id8_in.ap())]
    lo = p.sbuf("lo_r", [1, E], I32)
    hi = p.sbuf("hi_r", [1, E], I32)
    mid = p.sbuf("mid_r", [1, E], I32)
    gi = p.sbuf("gi_r", [1, E], I32)
    df = p.sbuf("df_r", [1, E], I32)
    thr = p.sbuf("thr_r", [1, E], F32)
    thr_bc = p.sbuf("thr_bc", [128, E], F32)
    ge = p.sbuf("ge", [128, E], F32)
    cnt = p.sbuf("cnt", [1, E], F32)
    cnti = p.sbuf("cnti", [1, E], I32)
    tprev = p.ops("vector", [
        lambda v: v.memset(lo[:], 0),
        lambda v: v.memset(hi[:], 0x3F800000),
    ])
    CAP = TT // E
    for it in range(31):
        t1 = p.ops("vector", [
            lambda v: v.tensor_add(mid[:], lo[:], hi[:]),
            lambda v: v.tensor_scalar(mid[:], mid[:], 1, None, op0=ALU.arith_shift_right),
            lambda v: v.tensor_copy(thr[:], mid[:].bitcast(F32)),
        ], wait=[tprev], chain=True)
        bi, ps, w8 = pool.acquire()
        t2 = p.op("tensor", lambda t, ps=ps: t.matmul(
            ps[0:128, 0:E], c["ones_row_f32"][0:1, 0:128],
            thr[0:1, 0:E], start=True, stop=True),
                  wait=[t1] + w8 + c["tok"])
        t3 = p.op("vector", lambda v, ps=ps: v.tensor_copy(thr_bc[:], ps[0:128, 0:E]),
                  wait=[t2])
        pool.release(bi, t3)
        bi2, ps2, w82 = pool.acquire()
        cmp_toks = []
        for tt in range(NTt):
            tc = p.op("vector", lambda v, tt=tt: v.tensor_tensor(
                ge[0:128, 0:E], pb[0:128, tt * E:(tt + 1) * E], thr_bc[0:128, 0:E],
                op=ALU.is_ge), wait=[t3] + pb_toks + (cmp_toks[-1:] if cmp_toks else []))
            tm = p.op("tensor", lambda t, tt=tt, ps2=ps2: t.matmul(
                ps2[0:1, 0:E], c["ones_col_f32"][0:128, 0:1], ge[0:128, 0:E],
                start=(tt == 0), stop=(tt == NTt - 1)), wait=[tc] + w82)
            cmp_toks.append(tm)
        t4 = p.op("vector", lambda v, ps2=ps2: v.tensor_copy(cnt[:], ps2[0:1, 0:E]),
                  wait=cmp_toks)
        pool.release(bi2, t4)
        tprev = p.ops("vector", [
            lambda v: v.tensor_scalar(cnt[:], cnt[:], float(CAP) - 0.5, None, op0=ALU.is_ge),
            lambda v: v.tensor_copy(cnti[:], cnt[:]),
            lambda v: v.tensor_sub(df[:], mid[:], lo[:]),
            lambda v: v.tensor_mul(df[:], df[:], cnti[:]),
            lambda v: v.tensor_add(lo[:], lo[:], df[:]),
            lambda v: v.tensor_sub(df[:], hi[:], mid[:]),
            lambda v: v.tensor_mul(df[:], df[:], cnti[:]),
            lambda v: v.tensor_add(hi[:], mid[:], df[:]),
        ], wait=[t4, t1], chain=True)
    tthr = p.op("vector", lambda v: v.tensor_copy(thr[:], lo[:].bitcast(F32)), wait=[tprev])
    bi, ps, w8 = pool.acquire()
    t2 = p.op("tensor", lambda t, ps=ps: t.matmul(
        ps[0:128, 0:E], c["ones_row_f32"][0:1, 0:128],
        thr[0:1, 0:E], start=True, stop=True),
              wait=[tthr] + w8)
    tthb = p.op("vector", lambda v, ps=ps: v.tensor_copy(thr_bc[:], ps[0:128, 0:E]),
                wait=[t2])
    pool.release(bi, tthb)
    NT = T // 128
    pown = p.sbuf("pown_sb", [128, NT * E], F32)
    po_toks = [p.dma("gpsimd", pown[0:128, tt * E:(tt + 1) * E],
                     apv(pown_in, tt * 128 * E, [[E, 128], [1, E]]))
               for tt in range(NT)]
    sel = p.sbuf("sel", [128, E], F32)
    sc2 = p.sbuf("sc2b", [128, E], F32)
    asg = p.sbuf("asg", [128, 1], F32)
    mxs = p.sbuf("mxs", [128, 1], F32)
    eqs = p.sbuf("eqs", [128, E], F32)
    idx = p.sbuf("idxq", [128, 1], F32)
    gates = p.sbuf("gates", [128, NT * E], BF16)
    gate_toks = []
    for tt in range(NT):
        pt_ = pown[0:128, tt * E:(tt + 1) * E]
        tg = p.ops("vector", [
            lambda v, pt_=pt_: v.tensor_tensor(sel[:], pt_, thr_bc[0:128, 0:E], op=ALU.is_ge),
            lambda v, pt_=pt_: v.tensor_scalar_add(sc2[:], pt_, 1.0),
            lambda v: v.tensor_mul(sc2[:], sc2[:], sel[:]),
            lambda v: v.tensor_scalar_add(sc2[:], sc2[:], -1.0),
            lambda v: v.tensor_reduce(asg[:, 0:1], sel[:], axis=AX.X, op=ALU.max),
            lambda v, pt_=pt_: v.tensor_sub(sc2[:], sc2[:], pt_),
            lambda v, pt_=pt_: v.scalar_tensor_tensor(sc2[:], sc2[:], asg[0:128, 0:1],
                                                      pt_, op0=ALU.mult, op1=ALU.add),
            lambda v: v.tensor_reduce(mxs[:, 0:1], sc2[:], axis=AX.X, op=ALU.max),
            lambda v: v.tensor_scalar(eqs[:], sc2[:], mxs[0:128, 0:1], None, op0=ALU.is_equal),
            lambda v: v.tensor_mul(eqs[:], eqs[:], em[0:128, 0:E]),
            lambda v: v.tensor_scalar_add(eqs[:], eqs[:], 999.0),
            lambda v: v.tensor_reduce(idx[:, 0:1], eqs[:], axis=AX.X, op=ALU.min),
            lambda v: v.tensor_scalar(eqs[:], erow[0:128, 0:E], idx[0:128, 0:1], None,
                                      op0=ALU.is_equal),
            lambda v, tt=tt: v.tensor_scalar(gates[0:128, tt * E:(tt + 1) * E], eqs[:],
                                             mxs[0:128, 0:1], None, op0=ALU.mult),
        ], wait=[tthb] + po_toks + cst + gate_toks[-1:], chain=True)
        gate_toks.append(tg)
    grow = p.sbuf("grow", [8, NT * 128], BF16)
    ptr = [p.psum(f"ptrm{i}", [128, 128], BF16) for i in range(2)]
    ptr_war = {}
    gr_toks = []
    for tt in range(NT):
        pst = ptr[tt % 2]
        war = ptr_war.pop(tt % 2, [])
        t1 = p.op("tensor", lambda t, tt=tt, pst=pst: t.transpose(
            pst[0:E, 0:128], gates[0:128, tt * E:(tt + 1) * E], idn[:]),
            wait=[gate_toks[tt]] + cst + war)
        t2 = p.op("vector", lambda v, tt=tt, pst=pst: v.tensor_copy(
            grow[0:8, tt * 128:(tt + 1) * 128], pst[0:E, 0:128]), wait=[t1])
        ptr_war[tt % 2] = [t2]
        gr_toks.append(t2)
    acc = p.sbuf("accm", [128, 8 * T], F32)
    tza = p.op("vector", lambda v: v.memset(acc[:], 0.0))
    acc_last = {}
    h4 = p.sbuf("h4m", [128, 2 * 16 * 512], BF16)
    gl = p.sbuf("glm", [128, 512], BF16)
    gbc = p.sbuf("gbc", [128, 512], BF16)
    grw = p.sbuf("grw", [1, 512], BF16)
    w1b = [p.sbuf(f"w1b{i}", [128, 8 * 128], BF16) for i in range(2)]
    w2b = [p.sbuf(f"w2b{i}", [128, 128], BF16) for i in range(2)]
    w1war = {0: [], 1: []}
    w2war = {0: [], 1: []}
    FC = F // 128
    for (s0, n) in ntiles(T, 512):
        for gg in range(4):
            h4_toks = []
            gbc_toks = []
            for ee in range(2):
                e = gg * 2 + ee
                bi, ps, w8 = pool.acquire()
                tx = p.op("tensor", lambda t, e=e, s0=s0, n=n, ps=ps: t.matmul(
                    ps[0:1, 0:n], id8b[0:8, e:e + 1], grow[0:8, s0:s0 + n],
                    start=True, stop=True), wait=gr_toks + cst + w8)
                tr = p.op("vector", lambda v, n=n, ps=ps: v.tensor_copy(
                    grw[0:1, 0:n], ps[0:1, 0:n]),
                          wait=[tx] + gbc_toks)
                pool.release(bi, tr)
                bi2, ps2, w82 = pool.acquire()
                tb = p.op("tensor", lambda t, n=n, ps2=ps2: t.matmul(
                    ps2[0:128, 0:n], c["ones_row_bf"][0:1, 0:128], grw[0:1, 0:n],
                    start=True, stop=True), wait=[tr] + w82)
                tb2 = p.op("vector", lambda v, n=n, ee=ee, ps2=ps2: v.tensor_copy(
                    gbc[0:128, 0:n] if ee == 0 else gl[0:128, 0:n], ps2[0:128, 0:n]),
                    wait=[tb] + h4_toks)
                pool.release(bi2, tb2)
                gbc_toks.append(tb2)
            gb2 = p.sbuf(f"gb2_{s0}_{gg}", [128, 1024], BF16) if (s0 == 0 and gg == 0) else p._gb2
            p._gb2 = gb2
            tgc = p.ops("vector", [
                lambda v, n=n: v.tensor_copy(gb2[0:128, 0:n], gbc[0:128, 0:n]),
                lambda v, n=n: v.tensor_copy(gb2[0:128, 512:512 + n], gl[0:128, 0:n]),
            ], wait=gbc_toks)
            for ee in range(2):
                e = gg * 2 + ee
                for m in range(FC):
                    pp = m % 2
                    wt = []
                    for kc in range(8):
                        srcw = apv(w1_in, e * D * F + kc * 128 * F + m * 128,
                                   [[F, 128], [1, 128]])
                        wt.append(p.dma("gpsimd", w1b[pp][0:128, kc * 128:(kc + 1) * 128],
                                        srcw, wait=w1war[pp] if kc == 0 else ()))
                    bi3, ps3, w83 = pool.acquire()
                    fns = []
                    for kc in range(8):
                        l = w1b[pp][0:128, kc * 128:(kc + 1) * 128]
                        rr = xn[0:128, kc * T + s0:kc * T + s0 + n]
                        fns.append(lambda t, a=ps3[0:128, 0:n], l=l, rr=rr,
                                   st=(kc == 0), sp=(kc == 7):
                                   t.matmul(a, l, rr, start=st, stop=sp))
                    tok_mm = p.ops("tensor", fns, wait=list(xn_toks) + wt + w83)
                    w1war[pp] = [tok_mm]
                    t_g = p.op("scalar", lambda s, o=gl[0:128, 0:n] if False else None,
                               i=None: None, sig=False) if False else None
                    glx = p.sbuf("glx", [128, 512], BF16) if (s0 == 0 and gg == 0 and ee == 0 and m == 0) else p._glx
                    p._glx = glx
                    t_g = p.op("scalar", lambda s, o=glx[0:128, 0:n], i=ps3[0:128, 0:n]:
                               s.activation(out=o, in_=i, func=AF.Gelu),
                               wait=[tok_mm] + h4_toks[-1:])
                    t_h = p.op("vector", lambda v, o=h4[0:128, (ee * 16 + m) * 512:(ee * 16 + m) * 512 + n],
                               i=glx[0:128, 0:n], j=gb2[0:128, ee * 512:ee * 512 + n]:
                               v.tensor_mul(o, i, j), wait=[t_g, tgc])
                    pool.release(bi3, t_h)
                    h4_toks.append(t_h)
            for mc in range(8):
                bi, ps, w8 = pool.acquire()
                last = None
                for j in range(2 * FC):
                    ee, k = j // FC, j % FC
                    e = gg * 2 + ee
                    pp2 = j % 2
                    wt2 = [p.dma("gpsimd", w2b[pp2][0:128, 0:128],
                                 apv(w2_in, e * F * D + k * 128 * D + mc * 128,
                                     [[D, 128], [1, 128]]), wait=w2war[pp2])]
                    l = w2b[pp2][0:128, 0:128]
                    rr = h4[0:128, (ee * 16 + k) * 512:(ee * 16 + k) * 512 + n]
                    last = p.op("tensor", lambda t, a=ps[0:128, 0:n], l=l, rr=rr,
                                st=(j == 0), sp=(j == 2 * FC - 1):
                                t.matmul(a, l, rr, start=st, stop=sp),
                                wait=h4_toks + wt2 + w8)
                    w2war[pp2] = [last]
                key = (mc, s0)
                war = acc_last.pop(key, [tza])
                t_add = p.op("vector", lambda v, o=acc[0:128, mc * T + s0:mc * T + s0 + n],
                             i=acc[0:128, mc * T + s0:mc * T + s0 + n], j=ps[0:128, 0:n]:
                             v.tensor_add(o, i, j), wait=[last] + war)
                pool.release(bi, t_add)
                acc_last[key] = [t_add]
    y_sb = [p.sbuf(f"ysbm{i}", [128, 512], F32) for i in range(2)]
    y_war = {}
    ctr = {"i": 0}
    for mc in range(8):
        for (s0, n) in ntiles(T, 512):
            xi_ = ctr["i"] % 2
            ctr["i"] += 1
            xap, xw = x_get(mc, s0, n)
            war = y_war.pop(xi_, [])
            t1 = p.op("vector", lambda v, o=y_sb[xi_][0:128, 0:n],
                      i=acc[0:128, mc * T + s0:mc * T + s0 + n], j=xap:
                      v.tensor_add(o, i, j),
                      wait=acc_last.get((mc, s0), []) + xw + war)
            x_get.note_reader(mc, s0, n, t1)
            dst = apv(y, mc * 128 * T + s0, [[T, 128], [1, n]])
            td = p.dma("sync", dst, y_sb[xi_][0:128, 0:n], wait=[t1], store=True)
            y_war[xi_] = [td]
    p.finish()
    return p.build()



def build_l1_attn_b_A(T=512, D=1024, DI=2048, HC=3):
    Ta = T + HC
    p = Prog()
    pool = PsumPool(p, 6)
    c = p.consts()
    x_in = p.din("x_in", [D, Ta])
    lnw = p.din("lnw_col", [128, 8])
    lnb = p.din("lnb_col", [128, 8])
    inw_t = p.din("inw_t", [D, 2 * DI])
    inb = p.din("inb_col", [128, 32])
    cw4 = p.din("convw4", [128, 32 * 4])
    cb = p.din("convb_col", [128, 32])
    aproj_t = p.din("aproj_t", [DI, 64])
    dtproj_t = p.din("dtproj_t", [64, DI])
    bdt = p.din("bdt_col", [128, 16])
    o_xs = p.dout("xs", [DI, T], BF16)
    o_z = p.dout("z", [DI, T], BF16)
    o_dtp = p.dout("dt_part", [128, 16])

    lnw_sb = p.sbuf("lnw_sb", [128, 8], F32)
    lnb_sb = p.sbuf("lnb_sb", [128, 8], F32)
    inb_sb = p.sbuf("inb_sb", [128, 32], F32)
    cw_sb = p.sbuf("cw_sb", [128, 128], F32)
    cb_sb = p.sbuf("cb_sb", [128, 32], F32)
    bdt_sb = p.sbuf("bdt_sb", [128, 16], F32)
    cst = [p.dma("gpsimd", lnw_sb[:], lnw.ap()), p.dma("gpsimd", lnb_sb[:], lnb.ap()),
           p.dma("gpsimd", inb_sb[:], inb.ap()), p.dma("gpsimd", cw_sb[:], cw4.ap()),
           p.dma("gpsimd", cb_sb[:], cb.ap()), p.dma("gpsimd", bdt_sb[:], bdt.ap())]
    x_get = make_x_loader(p, x_in, Ta, D, nbuf=10, eng="sync")
    xn = p.sbuf("xn", [128, 8 * Ta], BF16)
    xn_toks = emit_ln(p, pool, x_get, xn, Ta, D, g_col=lnw_sb, b_col=lnb_sb,
                      aff_toks=cst)
    xsz = p.sbuf("xsz", [128, 32 * Ta], BF16)
    wb = [p.sbuf(f"inwb{i}", [128, 8 * 128], BF16) for i in range(2)]
    wwar = {0: [], 1: []}
    xsz_toks = {}
    for mc in range(32):
        pp = mc % 2
        wt = []
        for kc in range(8):
            srcw = apv(inw_t, kc * 128 * 2 * DI + mc * 128, [[2 * DI, 128], [1, 128]])
            wt.append(p.dma("gpsimd", wb[pp][0:128, kc * 128:(kc + 1) * 128], srcw,
                            wait=wwar[pp] if kc == 0 else ()))
        mms = []
        for (s0, n) in ntiles(Ta, 512):
            bi, ps, w8 = pool.acquire()
            fns = []
            for kc in range(8):
                l = wb[pp][0:128, kc * 128:(kc + 1) * 128]
                rr = xn[0:128, kc * Ta + s0:kc * Ta + s0 + n]
                fns.append(lambda t, a=ps[0:128, 0:n], l=l, rr=rr,
                           st=(kc == 0), sp=(kc == 7): t.matmul(a, l, rr, start=st, stop=sp))
            tok_mm = p.ops("tensor", fns, wait=list(xn_toks) + wt + w8)
            te = p.op("vector", lambda v, o=xsz[0:128, mc * Ta + s0:mc * Ta + s0 + n],
                      i=ps[0:128, 0:n], s1=inb_sb[0:128, mc:mc + 1]:
                      v.tensor_scalar_add(o, i, s1), wait=[tok_mm] + cst)
            pool.release(bi, te)
            mms.append(tok_mm)
            xsz_toks.setdefault(mc, []).append(te)
        wwar[pp] = mms
    sc1 = p.sbuf("sc1", [128, T], F32)
    sc2 = p.sbuf("sc2", [128, T], F32)
    acc = [sc1, sc2]
    acc_war = {0: [], 1: []}
    ob = [p.sbuf(f"ob{i}", [128, 512], BF16) for i in range(3)]
    ob_war = {}
    ctr = {"i": 0}
    conv_toks = {}
    for ci in range(32):
        ai = ci % 2
        a = acc[ai]
        t0 = p.op("vector", lambda v, o=a[0:128, 0:T], i=xsz[0:128, ci * Ta:ci * Ta + T],
                  s1=cw_sb[0:128, ci * 4:ci * 4 + 1]:
                  v.tensor_single_scalar(o, i, s1, op=ALU.mult),
                  wait=xsz_toks[ci] + cst + acc_war[ai])
        for j in (1, 2, 3):
            t0 = p.op("vector", lambda v, o=a[0:128, 0:T],
                      i=xsz[0:128, ci * Ta + j:ci * Ta + j + T],
                      s1=cw_sb[0:128, ci * 4 + j:ci * 4 + j + 1], j2=a[0:128, 0:T]:
                      v.scalar_tensor_tensor(o, i, s1, j2, op0=ALU.mult, op1=ALU.add),
                      wait=[t0])
        t1 = p.op("vector", lambda v, o=a[0:128, 0:T], i=a[0:128, 0:T],
                  s1=cb_sb[0:128, ci:ci + 1]: v.tensor_scalar_add(o, i, s1), wait=[t0])
        t2 = p.op("scalar", lambda s, o=xsz[0:128, ci * Ta:ci * Ta + T], i=a[0:128, 0:T]:
                  s.activation(out=o, in_=i, func=AF.Silu), wait=[t1])
        acc_war[ai] = [t2]
        conv_toks[ci] = t2
        dst = o_xs if ci < 16 else o_z
        mcc = ci if ci < 16 else ci - 16
        for (s0, n) in ntiles(T, 512):
            oi = ctr["i"] % 3
            ctr["i"] += 1
            war = ob_war.pop(oi, [])
            t3 = p.op("vector", lambda v, o=ob[oi][0:128, 0:n],
                      i=xsz[0:128, ci * Ta + s0:ci * Ta + s0 + n]: v.tensor_copy(o, i),
                      wait=[t2] + war)
            td = p.dma("sync", apv(dst, mcc * 128 * T + s0, [[T, 128], [1, n]]),
                       ob[oi][0:128, 0:n], wait=[t3], store=True)
            ob_war[oi] = [td]
    ap_sb, ap_toks = load_weight(p, "ap_sb", aproj_t, DI, 64)
    ainp = p.sbuf("ainp", [64, T], BF16)
    ai_toks = []
    for (s0, n) in ntiles(T, 512):
        bi, ps, w8 = pool.acquire()
        fns = []
        for kc in range(16):
            l = ap_sb[0:128, kc * 64:(kc + 1) * 64]
            rr = xsz[0:128, kc * Ta + s0:kc * Ta + s0 + n]
            fns.append(lambda t, a=ps[0:64, 0:n], l=l, rr=rr,
                       st=(kc == 0), sp=(kc == 15): t.matmul(a, l, rr, start=st, stop=sp))
        tok_mm = p.ops("tensor", fns,
                       wait=[conv_toks[ci] for ci in range(16)] + ap_toks + w8)
        te = p.op("vector", lambda v, o=ainp[0:64, s0:s0 + n], i=ps[0:64, 0:n]:
                  v.tensor_copy(o, i), wait=[tok_mm])
        pool.release(bi, te)
        ai_toks.append(te)
    dtw = [p.sbuf(f"dtw{i}", [64, 128], BF16) for i in range(2)]
    dtwar = {0: [], 1: []}
    srow = p.sbuf("srow2", [128, 512], BF16)
    drow = p.sbuf("drow", [128, 512], F32)
    dtp = p.sbuf("dtp", [128, 16], F32)
    part = p.sbuf("part", [128, 1], F32)
    tzp = p.op("vector", lambda v: v.memset(dtp[:], 0.0))
    dtp_toks = [tzp]
    srow_war = []
    for mc in range(16):
        pp = mc % 2
        wt = [p.dma("gpsimd", dtw[pp][0:64, 0:128],
                    apv(dtproj_t, mc * 128, [[DI, 64], [1, 128]]), wait=dtwar[pp])]
        for (s0, n) in ntiles(T, 512):
            bi, ps, w8 = pool.acquire()
            tok_mm = p.op("tensor", lambda t, a=ps[0:128, 0:n], l=dtw[pp][0:64, 0:128],
                          rr=ainp[0:64, s0:s0 + n]: t.matmul(a, l, rr, start=True, stop=True),
                          wait=ai_toks + wt + w8)
            dtwar[pp] = [tok_mm]
            t1 = p.op("scalar", lambda s, o=srow[0:128, 0:n], i=ps[0:128, 0:n],
                      b=bdt_sb[0:128, mc:mc + 1]:
                      s.activation(out=o, in_=i, func=AF.Abs, bias=b),
                      wait=[tok_mm] + cst + srow_war)
            t2 = p.op("scalar", lambda s, o=srow[0:128, 0:n], i=srow[0:128, 0:n]:
                      s.activation(out=o, in_=i, func=AF.Exp, scale=-1.0), wait=[t1])
            t3 = p.op("scalar", lambda s, o=srow[0:128, 0:n], i=srow[0:128, 0:n]:
                      s.activation(out=o, in_=i, func=AF.Ln, bias=1.0), wait=[t2])
            t4 = p.op("vector", lambda v, o=drow[0:128, 0:n], i=ps[0:128, 0:n],
                      s1=bdt_sb[0:128, mc:mc + 1]:
                      v.tensor_scalar(o, i, s1, 0.0, op0=ALU.add, op1=ALU.max),
                      wait=[tok_mm, t1])
            t5 = p.op("vector", lambda v, o=drow[0:128, 0:n], i=drow[0:128, 0:n],
                      j=srow[0:128, 0:n]: v.tensor_add(o, i, j), wait=[t3, t4])
            pool.release(bi, t5)
            srow_war = [t5]
            t6 = p.op("vector", lambda v, o=part[0:128, 0:1], i=drow[0:128, 0:n]:
                      v.tensor_reduce(o, i, axis=AX.X, op=ALU.add), wait=[t5])
            t7 = p.op("vector", lambda v, o=dtp[0:128, mc:mc + 1],
                      i=dtp[0:128, mc:mc + 1], j=part[0:128, 0:1]: v.tensor_add(o, i, j),
                      wait=[t6] + dtp_toks[-1:])
            dtp_toks.append(t7)
    p.dma("sync", o_dtp.ap(), dtp[:], wait=dtp_toks[-1:], store=True)
    p.finish()
    return p.build()


def build_l1_attn_b_C(T=2048, D=1024, DI=2048):
    p = Prog()
    pool = PsumPool(p, 6)
    c = p.consts()
    x_in = p.din("x_in", [D, T])
    o_in = p.din("o_in", [DI, T], BF16)
    wout_t = p.din("wout_t", [DI, D])
    b_in = p.din("b_col", [128, 8])
    y = p.dout("y", [D, T])
    bcol = p.sbuf("bcol_sb", [128, 8], F32)
    cst = [p.dma("gpsimd", bcol[:], b_in.ap())]
    x_get = make_x_loader(p, x_in, T, D, nbuf=4, eng="sync")
    o_get = make_x_loader(p, o_in, T, DI, nbuf=20, tag="og", eng="gpsimd", dtype=BF16)
    y_sb = [p.sbuf(f"ysb{i}", [128, 512], F32) for i in range(2)]
    y_war = {}
    ctr = {"i": 0}

    def out_cb(mc, mrows, s0, n, ps, tok_mm):
        xi_ = ctr["i"] % 2
        ctr["i"] += 1
        xap, xw = x_get(mc, s0, n)
        war = y_war.pop(xi_, [])
        t1 = p.op("vector", lambda v, o=y_sb[xi_][0:mrows, 0:n], i=ps[0:mrows, 0:n],
                  s1=bcol[0:mrows, mc:mc + 1], j=xap:
                  v.scalar_tensor_tensor(o, i, s1, j, op0=ALU.add, op1=ALU.add),
                  wait=[tok_mm] + xw + war + cst)
        x_get.note_reader(mc, s0, n, t1)
        dst = apv(y, mc * 128 * T + s0, [[T, mrows], [1, n]])
        td = p.dma("sync", dst, y_sb[xi_][0:mrows, 0:n], wait=[t1], store=True)
        y_war[xi_] = [td]
        return t1

    emit_proj_stream_ld(p, pool, wout_t, D, o_get, T, DI, D, T, out_cb)
    p.finish()
    return p.build()


def build_l1_attn_b_B(NI=256, NC=128, Q=128, DS=16):
    NN = NI * DS
    p = Prog()
    pool = PsumPool(p, 6)
    c = p.consts()
    xs_in = p.din("xs_b", [NI * Q * NC], BF16)
    z_in = p.din("z_b", [NI * Q * NC], BF16)
    dtp_in = p.din("dt_parts", [32, NI])
    al_in = p.din("a_log_row", [1, NN])
    b_in = p.din("b_row", [1, NN])
    c_in = p.din("c_row", [1, NN])
    d_in = p.din("d_row", [1, NI])
    io_in = p.din("iota_row", [1, 128])
    o_out = p.dout("o_pre", [Q * NI * NC], BF16)
    vbuf = p.dint("vbuf", [NI * 255], BF16)

    xs_sb = p.sbuf("xs_sb", [128, NI * NC], BF16)
    xs_toks = []
    for b8 in range(8):
        srcx = apv(xs_in, b8 * 32 * Q * NC,
                   [[NC, Q], [Q * NC, 32], [1, NC]])
        xs_toks.append(p.dma("sync", sbv(xs_sb, NI * NC, 128, b8 * 32 * NC,
                                         [[NC, 32], [1, NC]]), srcx))
    dtp_sb = p.sbuf("dtp_sb", [32, NI], F32)
    alr = p.sbuf("alr", [1, NN], F32)
    brr = p.sbuf("brr", [1, NN], BF16)
    crr = p.sbuf("crr", [1, NN], BF16)
    drr = p.sbuf("drr", [1, NI], F32)
    ior = p.sbuf("ior", [1, 128], F32)
    cst = [p.dma("gpsimd", dtp_sb[:], dtp_in.ap()), p.dma("gpsimd", alr[:], al_in.ap()),
           p.dma("gpsimd", brr[:], b_in.ap()), p.dma("gpsimd", crr[:], c_in.ap()),
           p.dma("gpsimd", drr[:], d_in.ap()), p.dma("gpsimd", ior[:], io_in.ap())]
    dtq = p.sbuf("dtq", [1, NI], F32)
    bi, ps, w8 = pool.acquire()
    t1 = p.op("tensor", lambda t, ps=ps: t.matmul(
        ps[0:1, 0:NI], c["ones_col_f32"][0:32, 0:1],
        dtp_sb[0:32, 0:NI], start=True, stop=True),
              wait=cst + w8 + c["tok"])
    t2 = p.op("vector", lambda v, ps=ps: v.tensor_scalar_mul(
        dtq[:], ps[0:1, 0:NI], 1.0 / 16384.0),
              wait=[t1])
    pool.release(bi, t2)
    dtbc = apv(dtq, 0, [[NI, 1], [1, NI], [0, DS]])
    loga = p.sbuf("loga", [1, NN], F32)
    abar = p.sbuf("abar", [1, NN], BF16)
    aqr = p.sbuf("aqr", [1, NN], BF16)
    arow = p.sbuf("arow", [1, NN], BF16)
    ainv = p.sbuf("ainv", [1, NN], BF16)
    tmp1 = p.sbuf("tmp1", [1, NN], BF16)
    tmp2 = arow
    bbar = p.sbuf("bbar", [1, NN], BF16)
    cur = p.sbuf("cur", [1, NN], BF16)
    ta = p.op("scalar", lambda s: s.activation(out=arow[:], in_=alr[:], func=AF.Exp),
              wait=cst)
    tb = p.op("vector", lambda v: v.tensor_scalar_mul(arow[:], arow[:], -1.0), wait=[ta])
    tc = p.op("vector", lambda v: v.tensor_mul(loga[:], arow[:], dtbc), wait=[tb, t2])
    td = p.op("scalar", lambda s: s.activation(out=abar[:], in_=loga[:], func=AF.Exp),
              wait=[tc])
    te = p.op("scalar", lambda s: s.activation(out=aqr[:], in_=loga[:], func=AF.Exp,
                                               scale=float(Q)), wait=[tc])
    tf = p.ops("vector", [
        lambda v: v.tensor_scalar_add(ainv[:], arow[:], 1e-8),
        lambda v: v.reciprocal(ainv[:], ainv[:]),
    ], wait=[tb], chain=True)
    tg = p.op("vector", lambda v: v.tensor_scalar_add(tmp1[:], abar[:], -1.0), wait=[td])
    th = p.op("vector", lambda v: v.tensor_mul(tmp1[:], tmp1[:], ainv[:]), wait=[tg, tf])
    ti = p.op("vector", lambda v: v.tensor_copy(tmp2[:], loga[:]), wait=[tc, tf])
    tj = p.op("scalar", lambda s: s.activation(out=tmp2[:], in_=tmp2[:], func=AF.Abs),
              wait=[ti])
    tk = p.ops("vector", [
        lambda v: v.tensor_scalar(tmp2[:], tmp2[:], 1e-6, None, op0=ALU.is_lt),
        lambda v: v.tensor_sub(ainv[:], dtbc, tmp1[:]),
        lambda v: v.tensor_mul(ainv[:], ainv[:], tmp2[:]),
        lambda v: v.tensor_add(tmp1[:], tmp1[:], ainv[:]),
        lambda v: v.tensor_mul(bbar[:], tmp1[:], brr[:]),
        lambda v: v.tensor_mul(cur[:], abar[:], crr[:]),
    ], wait=[tj, th] + cst, chain=True)
    powt = p.sbuf("powt", [128, NN], BF16)
    pow2 = p.sbuf("pow2", [128, NN], BF16)
    wacc = p.sbuf("wacc", [128, NI], F32)
    w_bf = p.sbuf("w_bf", [128, NI], BF16)
    bbc = pow2
    pw_toks = []
    for (s0, n) in ntiles(NN, 512):
        bi, ps, w8 = pool.acquire()
        tm = p.op("tensor", lambda t, s0=s0, n=n, ps=ps: t.matmul(
            ps[0:128, 0:n], ior[0:1, 0:128], loga[0:1, s0:s0 + n],
            start=True, stop=True), wait=[tk] + cst + w8)
        tev = p.op("scalar", lambda s, s0=s0, n=n, ps=ps: s.activation(
            out=powt[0:128, s0:s0 + n], in_=ps[0:128, 0:n], func=AF.Exp), wait=[tm])
        pool.release(bi, tev)
        bi2, ps2, w82 = pool.acquire()
        tb2 = p.op("tensor", lambda t, s0=s0, n=n, ps2=ps2: t.matmul(
            ps2[0:128, 0:n], c["ones_row_bf"][0:1, 0:128], bbar[0:1, s0:s0 + n],
            start=True, stop=True), wait=[tk] + w82)
        tb3 = p.op("vector", lambda v, s0=s0, n=n, ps2=ps2: v.tensor_copy(
            bbc[0:128, s0:s0 + n], ps2[0:128, 0:n]), wait=[tb2])
        pool.release(bi2, tb3)
        tb4 = p.op("vector", lambda v, s0=s0, n=n: v.tensor_mul(
            pow2[0:128, s0:s0 + n], powt[0:128, s0:s0 + n], bbc[0:128, s0:s0 + n]),
            wait=[tev, tb3])
        pw_toks.append(tb4)
    cb_toks = []
    for (s0, n) in ntiles(NN, 512):
        bi2, ps2, w82 = pool.acquire()
        tb2 = p.op("tensor", lambda t, s0=s0, n=n, ps2=ps2: t.matmul(
            ps2[0:128, 0:n], c["ones_row_bf"][0:1, 0:128], crr[0:1, s0:s0 + n],
            start=True, stop=True), wait=[tk] + pw_toks + w82)
        tb3 = p.op("vector", lambda v, s0=s0, n=n, ps2=ps2: v.tensor_copy(
            powt[0:128, s0:s0 + n], ps2[0:128, 0:n]), wait=[tb2] + pw_toks)
        pool.release(bi2, tb3)
        cb_toks.append(tb3)
    tw0 = p.op("vector", lambda v: v.memset(wacc[:], 0.0))
    tmp128 = p.sbuf("tmp128", [128, NI], BF16)
    twp = tw0
    for n_ in range(DS):
        twm = p.op("vector", lambda v, n_=n_: v.tensor_tensor(
            tmp128[0:128, 0:NI], sbv(pow2, NN, 128, n_, [[DS, NI]]),
            sbv(powt, NN, 128, n_, [[DS, NI]]), op=ALU.mult),
            wait=[twp] + pw_toks + cb_toks)
        twp = p.op("vector", lambda v: v.tensor_add(wacc[0:128, 0:NI],
                                                    wacc[0:128, 0:NI],
                                                    tmp128[0:128, 0:NI]),
                   wait=[twm])
    twb = p.op("vector", lambda v: v.tensor_copy(w_bf[:], wacc[:]), wait=[twp])
    idn2 = p.sbuf("idn2", [128, 128], BF16)
    id_in2 = p.din("id128b", [128, 128])
    tid = p.dma("gpsimd", idn2[:], id_in2.ap())
    zt = p.sbuf("zt", [128, 127], BF16)
    tz2 = p.op("vector", lambda v: v.memset(zt[:], 0.0))
    vb_toks = []
    psb16 = [p.psum(f"psb16_{i}", [128, 128], BF16) for i in range(2)]
    wT = p.sbuf("wT", [128, 256], BF16)
    prev = []
    for half in range(2):
        tt1 = p.op("tensor", lambda t, half=half: t.transpose(
            psb16[half][0:128, 0:128], w_bf[0:128, half * 128:(half + 1) * 128],
            idn2[:]), wait=[twb, tid] + prev)
        tt2 = p.op("vector", lambda v, half=half: v.tensor_copy(
            wT[0:128, half * 128:(half + 1) * 128], psb16[half][0:128, 0:128]),
            wait=[tt1])
        prev = [tt2]
        vb_toks.append(tt2)
    dvb = []
    for half in range(2):
        dst = apv(vbuf, (half * 128) * 255 + 127, [[255, 128], [1, 128]])
        dvb.append(p.dma("sync", dst, wT[0:128, half * 128:(half + 1) * 128],
                         wait=vb_toks))
        dstz = apv(vbuf, (half * 128) * 255, [[255, 128], [1, 127]])
        dvb.append(p.dma("sync", dstz, zt[0:128, 0:127], wait=[tz2]))
    toep = [p.sbuf(f"toep{i}", [128, 8 * 128], BF16) for i in range(2)]
    zb = [p.sbuf(f"zb{i}", [128, 8 * 128], BF16) for i in range(1)]
    zb = [zb[0], zb[0]]
    g_f = p.sbuf("g_f", [16, 128], F32)
    h_sb = p.sbuf("h_sb", [16, 128], F32)
    h_bf = p.sbuf("h_bf", [16, 128], BF16)
    aqc = p.sbuf("aqc", [16, 1], F32)
    cuc = p.sbuf("cuc", [16, 1], F32)
    lgc = p.sbuf("lgc", [16, 1], F32)
    ui = p.sbuf("ui", [16, 128], BF16)
    uexp = p.sbuf("uexp", [16, 128], BF16)
    obuf = [p.sbuf(f"obuf{i}", [128, 128], BF16) for i in range(2)]
    dbc = p.sbuf("dbc", [128, NI], F32)
    bi, ps, w8 = pool.acquire()
    tdp = p.op("vector", lambda v: v.tensor_scalar_add(drr[:], drr[:], 1.0), wait=cst)
    tdb = p.op("tensor", lambda t, ps=ps: t.matmul(
        ps[0:128, 0:NI], c["ones_row_f32"][0:1, 0:128],
        drr[0:1, 0:NI], start=True, stop=True),
               wait=[tdp] + w8)
    tdc = p.op("vector", lambda v, ps=ps: v.tensor_copy(dbc[:], ps[0:128, 0:NI]),
               wait=[tdb])
    pool.release(bi, tdc)
    toep_war = {0: [], 1: []}
    zb_war = {0: [], 1: []}
    ob_war = {}
    octr = {"i": 0}
    prev_small = []
    for i in range(NI):
        blk = i // 8
        pp = blk % 2
        if i % 8 == 0:
            srct = apv(vbuf, i * 255, [[1, 128], [255, 8], [1, 128]])
            ttl = p.dma("gpsimd", toep[pp][0:128, 0:8 * 128], srct,
                        wait=dvb + toep_war[pp])
            toep_war[pp] = [ttl]
            srcz = apv(z_in, i * Q * NC, [[NC, Q], [Q * NC, 8], [1, NC]])
            tzl = p.dma("sync", zb[pp][0:128, 0:8 * 128], srcz,
                        wait=zb_war[0] + zb_war[1])
            zb_war[0] = [tzl]
            zb_war[1] = [tzl]
        ttl = toep_war[pp][0]
        tzl = zb_war[pp][0]
        bi2, ps2, w82 = pool.acquire()
        tx1 = p.op("tensor", lambda t, i=i, ps2=ps2: t.matmul(
            ps2[0:16, 0:1], aqr[0:1, i * DS:(i + 1) * DS], c["ones_row_bf"][0:1, 0:1],
            start=True, stop=True), wait=[te] + w82 + prev_small)
        tx2 = p.op("vector", lambda v, ps2=ps2: v.tensor_copy(aqc[:], ps2[0:16, 0:1]),
                   wait=[tx1])
        pool.release(bi2, tx2)
        bi3, ps3, w83 = pool.acquire()
        tx3 = p.op("tensor", lambda t, i=i, ps3=ps3: t.matmul(
            ps3[0:16, 0:128], loga[0:1, i * DS:(i + 1) * DS], ior[0:1, 0:128],
            start=True, stop=True), wait=[tk] + w83)
        tx4 = p.op("scalar", lambda s, ps3=ps3: s.activation(
            out=uexp[0:16, 0:128], in_=ps3[0:16, 0:128], func=AF.Exp), wait=[tx3])
        pool.release(bi3, tx4)
        bi4, ps4, w84 = pool.acquire()
        tx5 = p.op("tensor", lambda t, i=i, ps4=ps4: t.matmul(
            ps4[0:16, 0:1], cur[0:1, i * DS:(i + 1) * DS], c["ones_row_bf"][0:1, 0:1],
            start=True, stop=True), wait=[tk] + w84)
        tx6 = p.op("vector", lambda v, ps4=ps4: v.tensor_copy(cuc[:], ps4[0:16, 0:1]),
                   wait=[tx5])
        pool.release(bi4, tx6)
        tu = p.op("vector", lambda v: v.tensor_single_scalar(
            ui[0:16, 0:128], uexp[0:16, 0:128], cuc[0:16, 0:1], op=ALU.mult),
            wait=[tx4, tx6])
        biA, psA, w8A = pool.acquire()
        biB, psB, w8B = pool.acquire()
        rhs = xs_sb[0:128, i * NC:(i + 1) * NC]
        tm1 = p.op("tensor", lambda t, i=i, psA=psA, pp=pp, rhs=rhs: t.matmul(
            psA[0:128, 0:NC], toep[pp][0:128, (i % 8) * 128:(i % 8) * 128 + 128],
            rhs, start=True, stop=False), wait=[ttl] + xs_toks + w8A + w8B)
        tm2 = p.op("tensor", lambda t, i=i, psB=psB, rhs=rhs: t.matmul(
            psB[0:16, 0:NC], pow2[0:128, i * DS:(i + 1) * DS], rhs,
            start=True, stop=True), wait=pw_toks + [tm1], sig=True)
        tg1 = p.op("vector", lambda v, psB=psB: v.tensor_copy(g_f[0:16, 0:128],
                                                              psB[0:16, 0:NC]),
                   wait=[tm2])
        pool.release(biB, tg1)
        ts1 = p.op("vector", lambda v: v.tensor_tensor_scan(
            out=h_sb[0:16, 0:64], data0=apv2_aq(aqc, 64), data1=g_f[0:16, 0:64],
            initial=0.0, op0=ALU.mult, op1=ALU.add), wait=[tg1, tx2])
        ts2 = p.op("vector", lambda v: v.tensor_tensor_scan(
            out=h_sb[0:16, 64:128], data0=apv2_aq(aqc, 64), data1=g_f[0:16, 64:128],
            initial=0.0, op0=ALU.mult, op1=ALU.add), wait=[ts1])
        tsb = p.op("vector", lambda v: v.tensor_copy(h_bf[:], h_sb[:]), wait=[ts2])
        tm3 = p.op("tensor", lambda t, psA=psA: t.matmul(
            psA[0:128, 1:64], ui[0:16, 0:128], h_bf[0:16, 0:63],
            start=False, stop=False), wait=[tu, tsb])
        tm4 = p.op("tensor", lambda t, psA=psA: t.matmul(
            psA[0:128, 65:128], ui[0:16, 0:128], h_bf[0:16, 64:127],
            start=False, stop=True), wait=[tm3, tsb])
        oi = octr["i"] % 2
        octr["i"] += 1
        owar = ob_war.pop(oi, [])
        tv1 = p.op("vector", lambda v, i=i, oi=oi, psA=psA, pp=pp: v.tensor_mul(
            obuf[oi][0:128, 0:128], psA[0:128, 0:NC],
            zb[pp][0:128, (i % 8) * 128:(i % 8) * 128 + 128]),
            wait=[tm4, tzl, tdc] + owar)
        pool.release(biA, tv1)
        zb_war[0] = zb_war[0] + [tv1]
        zb_war[1] = zb_war[1] + [tv1]
        tv2 = p.op("vector", lambda v, i=i, oi=oi: v.tensor_single_scalar(
            obuf[oi][0:128, 0:128], obuf[oi][0:128, 0:128], dbc[0:128, i:i + 1],
            op=ALU.mult), wait=[tv1])
        dsto = apv(o_out, i * NC, [[NI * NC, Q], [1, NC]])
        tdo = p.dma("sync", dsto, obuf[oi][0:128, 0:128], wait=[tv2], store=True)
        ob_war[oi] = [tdo]
        prev_small = [tx2, tx6]
    p.finish()
    return p.build()


def apv2_aq(t, n):
    ap = t.ap().copy()
    ap.offset = 0
    ap.ap = vp([[1, 16], [0, n]])
    return ap
import numpy as np
import ml_dtypes as _mld
from concourse.bass_utils import run_bass_kernel_spmd

_BF = _mld.bfloat16
_PROG_CACHE = {}
_STAGE_HOOK = None


def _snap(name, x):
    if _STAGE_HOOK is not None:
        _STAGE_HOOK(name, x)


def _prog(key, fn):
    if key not in _PROG_CACHE:
        _PROG_CACHE[key] = fn()
    return _PROG_CACHE[key]


def _fm(a):
    return np.ascontiguousarray(a.T)


def _halo_slice(x2, g0, g1):
    L, D = x2.shape
    lo, hi = max(0, g0), min(L, g1)
    out = np.zeros((g1 - g0, D), np.float32)
    if hi > lo:
        out[lo - g0:hi - g0] = x2[lo:hi]
    return out


def _run(nc, maps):
    res = run_bass_kernel_spmd(nc, maps, core_ids=list(range(len(maps))))
    return res.results


def kernel(**I):
    B, L, D = 2, 8192, 1024
    T = 2048
    x = np.asarray(I["L0_attn_a_x"], np.float32).copy()

    def core(c):
        return c // 4, (c % 4) * T

    nc = _prog("a0", lambda: build_l0_attn_a(T=T, H=256))
    w_qkv = np.ascontiguousarray(I["L0_attn_a_w_qkv"], np.float32)
    w_out = np.ascontiguousarray(I["L0_attn_a_w_out"], np.float32)
    maps = []
    for c in range(8):
        b, s0 = core(c)
        x_in = _fm(_halo_slice(x[b], s0 - 256, s0 + T))
        maps.append(dict(x_in=x_in, w_qkv=w_qkv, w_out=w_out,
                         mask=l0_attn_a_host_mask(T, 256, (s0 == 0))))
    rs = _run(nc, maps)
    for c in range(8):
        b, s0 = core(c)
        x[b, s0:s0 + T] = rs[c]["y"].T

    _snap("s1_attn_a", x)

    zeros_base = np.zeros((D, T), np.float32)

    def run_l0_ffn():
        base = x.copy()
        y = x
        for l in range(2):
            wq = np.ascontiguousarray(I["L0_ffn_wqkv"][l], np.float32)
            qs = np.asarray(I["L0_ffn_q_scale"][l], np.float32)
            ks = np.asarray(I["L0_ffn_k_scale"][l], np.float32)
            wo = np.ascontiguousarray(I["L0_ffn_wo"][l], np.float32)
            nc_a = _prog("f0a" if l == 0 else "f0a_noln",
                         lambda l=l: build_l0_ffn_a(T=1024, ln=(l == 0)))
            xmids = {}
            for h in range(2):
                maps = []
                for c in range(8):
                    b, s0 = core(c)
                    g0 = s0 + h * 1024
                    x_in = _fm(_halo_slice(y[b], g0 - 768, g0 + 1024))
                    hp = l0_ffn_a_host(wq, qs, ks, L, g0, T=1024, HQ=128, H=768)
                    maps.append(dict(x_in=x_in, wqkv=wq, wo=wo, **hp))
                rs = _run(nc_a, maps)
                for c in range(8):
                    xmids[(c, h)] = rs[c]["xmid"]
            nc_b = _prog("f0b", lambda: build_l0_ffn_b(T=T))
            hb = l0_ffn_b_host(I["L0_ffn_ln_g"][l], I["L0_ffn_ln_b"][l],
                               I["L0_ffn_b1"][l], I["L0_ffn_b2"][l])
            w1 = np.ascontiguousarray(I["L0_ffn_w1"][l], np.float32)
            w2 = np.ascontiguousarray(I["L0_ffn_w2"][l], np.float32)
            maps = []
            for c in range(8):
                b, s0 = core(c)
                h0, h1 = xmids[(c, 0)], xmids[(c, 1)]
                xm = np.concatenate([h0[:, 127:1152], h1[:, 128:1152]], axis=1)
                bs = zeros_base if l == 0 else _fm(base[b, s0:s0 + T])
                maps.append(dict(xm_in=np.ascontiguousarray(xm), base_in=bs,
                                 w1=w1, w2=w2, **hb))
            rs = _run(nc_b, maps)
            ynew = np.empty_like(base)
            for c in range(8):
                b, s0 = core(c)
                ynew[b, s0:s0 + T] = rs[c]["y"].T
            y = ynew
        x[:] = y

    run_l0_ffn()
    _snap("s2_ffn0_inv1", x)

    nc_a = _prog("b0a", lambda: build_l0_attn_b_A(T=T))
    base = dict(
        w_in_t=np.ascontiguousarray(I["L0_attn_b_W_in"].T.astype(np.float32)),
        conv_w4=np.ascontiguousarray(
            np.asarray(I["L0_attn_b_conv_w"], np.float32)[:, 0, :]
            .reshape(16, 128, 4).transpose(1, 0, 2).reshape(128, 64)),
        conv_b_col=col_layout(np.asarray(I["L0_attn_b_conv_b"], np.float32), 16),
        wx_dt=np.ascontiguousarray(I["L0_attn_b_W_x"][:64].T.astype(np.float32)),
        wx_B=np.ascontiguousarray(I["L0_attn_b_W_x"][64:192].T.astype(np.float32)),
        wx_C=np.ascontiguousarray(I["L0_attn_b_W_x"][192:].T.astype(np.float32)),
        wdt_t=np.ascontiguousarray(I["L0_attn_b_W_dt"].T.astype(np.float32)),
        bdt_col=np.asarray(I["L0_attn_b_b_dt"], np.float32).reshape(128, 1).copy(),
        alog_col=np.asarray(I["L0_attn_b_A_log"], np.float32).reshape(128, 1).copy())
    maps = []
    for c in range(8):
        b, s0 = core(c)
        x_in = _fm(_halo_slice(x[b], s0 - 3, s0 + T))
        maps.append(dict(x_in=np.ascontiguousarray(x_in), **base))
    rs = _run(nc_a, maps)
    ps_all = np.ascontiguousarray(np.stack([rs[c]["pend"][:, 0] for c in range(8)], axis=1))
    ss_all = np.ascontiguousarray(np.stack([rs[c]["send"][:, 0] for c in range(8)], axis=1))
    nc_b = _prog("b0b", lambda: build_l0_attn_b_B(T=T))
    maps_b = []
    for c in range(8):
        b, s0 = core(c)
        qc = c % 4
        gfl = np.zeros((128, 8), np.float32)
        gfl[:, 4 * b:4 * b + qc] = 1.0
        maps_b.append(dict(
            x_in=_fm(x[b, s0:s0 + T]), xisilu=rs[c]["xisilu"], y0=rs[c]["y0"],
            cp=rs[c]["cp"], ps_all=ps_all, ss_all=ss_all, gfl=gfl, omg=1.0 - gfl,
            w_gate=np.asarray(I["L0_attn_b_w_gate"], np.float32).reshape(1, 1).copy(),
            w_out_t=np.ascontiguousarray(I["L0_attn_b_W_out"].T.astype(np.float32))))
    rs_b = _run(nc_b, maps_b)
    for c in range(8):
        b, s0 = core(c)
        x[b, s0:s0 + T] = rs_b[c]["y"].T

    _snap("s3_attn_b", x)

    run_l0_ffn()
    _snap("s4_ffn0_inv2", x)

    nc_a = _prog("a1a", lambda: build_l1_attn_a_A(T=1024))
    k3 = np.asarray(I["L1_attn_a_conv_w"], np.float32)[:, 0, :, 1]
    conv3_col = np.ascontiguousarray(
        k3.reshape(8, 128, 3).transpose(1, 0, 2).reshape(128, 24))
    pe_w_t = np.ascontiguousarray(I["L1_attn_a_pe_w"].T.astype(np.float32))
    qkv_w_t = np.ascontiguousarray(I["L1_attn_a_qkv_w"].T.astype(np.float32))
    out_w_t = np.ascontiguousarray(I["L1_attn_a_out_w"].T.astype(np.float32))
    qh, kh, vh = {}, {}, {}
    for h in range(2):
        maps = []
        for c in range(8):
            b, s0 = core(c)
            g0 = s0 + h * 1024
            x_in = _fm(_halo_slice(x[b], g0 - 1, g0 + 1024 + 1))
            maps.append(dict(x_in=x_in, conv3_col=conv3_col, pe_w_t=pe_w_t,
                             qkv_w_t=qkv_w_t))
        rs = _run(nc_a, maps)
        for c in range(8):
            qh[(c, h)] = rs[c]["q_f"]
            kh[(c, h)] = rs[c]["k_f"]
            vh[(c, h)] = rs[c]["v_t"]
    nc_b = _prog("a1b", lambda: build_l1_attn_a_B(T=T, S=L))
    K_f, V_t = {}, {}
    for b in range(2):
        K_f[b] = np.ascontiguousarray(np.concatenate(
            [kh[(4 * b + qc, h)] for qc in range(4) for h in range(2)], axis=1))
        V_t[b] = np.ascontiguousarray(np.concatenate(
            [vh[(4 * b + qc, h)] for qc in range(4) for h in range(2)], axis=0))
    maps = []
    for c in range(8):
        b, s0 = core(c)
        q_f = np.ascontiguousarray(np.concatenate([qh[(c, 0)], qh[(c, 1)]], axis=1))
        maps.append(dict(q_f=q_f, k_f=K_f[b], v_t=V_t[b],
                         x_in=_fm(x[b, s0:s0 + T]), out_w_t=out_w_t))
    rs = _run(nc_b, maps)
    for c in range(8):
        b, s0 = core(c)
        x[b, s0:s0 + T] = rs[c]["y"].T

    _snap("s5_attn_a1", x)

    E = 8
    erow = np.ascontiguousarray(np.broadcast_to(
        np.arange(E, dtype=np.float32), (128, E)))
    em999 = np.ascontiguousarray(erow - 999.0)
    id128 = np.eye(128, dtype=np.float32)
    id8 = np.eye(8, dtype=np.float32)

    def run_l1_ffn():
        nc_a = _prog("f1a", lambda: build_l1_ffn_A(T=T))
        wg = np.ascontiguousarray(I["L1_ffn_Wg"], np.float32)
        maps = []
        for c in range(8):
            b, s0 = core(c)
            maps.append(dict(x_in=_fm(x[b, s0:s0 + T]), wg=wg))
        rs = _run(nc_a, maps)
        probs_all = np.ascontiguousarray(
            np.concatenate([rs[c]["probs"] for c in range(8)], axis=0))
        nc_b = _prog("f1b", lambda: build_l1_ffn_B(T=1024))
        w1 = np.ascontiguousarray(I["L1_ffn_W1"], np.float32)
        w2 = np.ascontiguousarray(I["L1_ffn_W2"], np.float32)
        ys = {}
        for h in range(2):
            maps = []
            for c in range(8):
                b, s0 = core(c)
                g0 = s0 + h * 1024
                po = np.ascontiguousarray(probs_all[c * T + h * 1024:c * T + h * 1024 + 1024])
                maps.append(dict(x_in=_fm(x[b, g0:g0 + 1024]), probs_all=probs_all,
                                 probs_own=po, w1=w1, w2=w2, e_row=erow,
                                 em999=em999, id128=id128, id8=id8))
            rs2 = _run(nc_b, maps)
            for c in range(8):
                ys[(c, h)] = rs2[c]["y"]
        for c in range(8):
            b, s0 = core(c)
            x[b, s0:s0 + T] = np.concatenate([ys[(c, 0)], ys[(c, 1)]], axis=1).T

    run_l1_ffn()
    _snap("s6_ffn1_inv1", x)

    DI = 2048
    nc_a = _prog("b1a", lambda: build_l1_attn_b_A(T=512))
    cwx = np.asarray(I["L1_attn_b_conv_w"], np.float32)[:, 0, :]
    cwz = np.asarray(I["L1_attn_b_gate_w"], np.float32)[:, 0, :]
    cw_all = np.concatenate([cwx, cwz], axis=0)
    convw4 = np.ascontiguousarray(
        cw_all.reshape(32, 128, 4).transpose(1, 0, 2).reshape(128, 128))
    cb_all = np.concatenate([np.asarray(I["L1_attn_b_conv_b"], np.float32),
                             np.asarray(I["L1_attn_b_gate_b"], np.float32)])
    baseA = dict(
        lnw_col=col_layout(np.asarray(I["L1_attn_b_ln_w"], np.float32), 8),
        lnb_col=col_layout(np.asarray(I["L1_attn_b_ln_b"], np.float32), 8),
        inw_t=np.ascontiguousarray(I["L1_attn_b_in_proj_w"].T.astype(np.float32)),
        inb_col=col_layout(np.asarray(I["L1_attn_b_in_proj_b"], np.float32), 32),
        convw4=convw4, convb_col=col_layout(cb_all, 32),
        aproj_t=np.ascontiguousarray(I["L1_attn_b_A_proj_w"].T.astype(np.float32)),
        dtproj_t=np.ascontiguousarray(I["L1_attn_b_dt_proj_w"].T.astype(np.float32)),
        bdt_col=col_layout(np.asarray(I["L1_attn_b_dt_proj_b"], np.float32), 16))
    xs_g = np.zeros((16384, DI), _BF)
    z_g = np.zeros((16384, DI), _BF)
    dtp_list = []
    for q4 in range(4):
        maps = []
        for c in range(8):
            b, s0 = core(c)
            g0 = s0 + q4 * 512
            maps.append(dict(x_in=_fm(_halo_slice(x[b], g0 - 3, g0 + 512)), **baseA))
        rs = _run(nc_a, maps)
        for c in range(8):
            b, s0 = core(c)
            gt = b * L + s0 + q4 * 512
            xs_g[gt:gt + 512] = rs[c]["xs"].T
            z_g[gt:gt + 512] = rs[c]["z"].T
            dtp_list.append(np.asarray(rs[c]["dt_part"], np.float32).T.reshape(DI))
    dtp_full = np.stack(dtp_list, axis=0)
    nc_b = _prog("b1b", lambda: build_l1_attn_b_B())
    AL = np.asarray(I["L1_attn_b_A_log"], np.float32)
    BM = np.asarray(I["L1_attn_b_B_mat"], np.float32)
    CM = np.asarray(I["L1_attn_b_C_mat"], np.float32)
    DV = np.asarray(I["L1_attn_b_D_vec"], np.float32)
    iota = np.arange(128, dtype=np.float32).reshape(1, 128).copy()
    xs_cic = xs_g.T.reshape(DI, 128, 128).transpose(0, 2, 1)
    xs_rev = np.ascontiguousarray(xs_cic[:, ::-1, :])
    z_cic = np.ascontiguousarray(z_g.T.reshape(DI, 128, 128).transpose(0, 2, 1))
    maps = []
    for c in range(8):
        i0 = c * 256
        maps.append(dict(
            xs_b=np.ascontiguousarray(xs_rev[i0:i0 + 256]).reshape(-1),
            z_b=np.ascontiguousarray(z_cic[i0:i0 + 256]).reshape(-1),
            dt_parts=np.ascontiguousarray(dtp_full[:, i0:i0 + 256]),
            a_log_row=np.ascontiguousarray(AL[i0:i0 + 256].reshape(1, -1)),
            b_row=np.ascontiguousarray(BM[i0:i0 + 256].reshape(1, -1)),
            c_row=np.ascontiguousarray(CM[i0:i0 + 256].reshape(1, -1)),
            d_row=np.ascontiguousarray(DV[i0:i0 + 256].reshape(1, -1)),
            iota_row=iota, id128b=id128))
    rs = _run(nc_b, maps)
    o_g = np.zeros((16384, DI), _BF)
    for c in range(8):
        i0 = c * 256
        op = np.asarray(rs[c]["o_pre"]).reshape(128, 256, 128)
        o_g[:, i0:i0 + 256] = op.transpose(2, 0, 1).reshape(16384, 256)
    nc_c = _prog("b1c", lambda: build_l1_attn_b_C(T=T))
    wout_t = np.ascontiguousarray(I["L1_attn_b_out_proj_w"].T.astype(np.float32))
    b_col = col_layout(np.asarray(I["L1_attn_b_out_proj_b"], np.float32), 8)
    maps = []
    for c in range(8):
        b, s0 = core(c)
        gt = b * L + s0
        o_in = np.ascontiguousarray(o_g[gt:gt + T].T)
        maps.append(dict(x_in=_fm(x[b, s0:s0 + T]), o_in=o_in, wout_t=wout_t,
                         b_col=b_col))
    rs = _run(nc_c, maps)
    for c in range(8):
        b, s0 = core(c)
        x[b, s0:s0 + T] = rs[c]["y"].T

    _snap("s7_attn_b1", x)

    run_l1_ffn()
    _snap("s8_ffn1_inv2", x)

    return x
